# version 12
# baseline (speedup 1.0000x reference)
import numpy as np
import concourse.bass as bass
import concourse.mybir as mybir
from concourse.bass_utils import run_bass_kernel_spmd
from contextlib import ExitStack

F32 = mybir.dt.float32
BF16 = mybir.dt.bfloat16
U8 = mybir.dt.uint8
U32 = mybir.dt.uint32
AF = mybir.ActivationFunctionType
ALU = mybir.AluOpType
AX = mybir.AxisListType

B, T, D = 4, 4096, 1024
DI = 2048
NS = 16
E, H = 8, 4096
NTOK = B * T
TOK = NTOK // 8
NT = TOK // 128
L = 64
NCH = TOK // L
CAP = 5120
EPS_TOP = 1.1920929e-07
LN_EPS = 1e-5
RMS_EPS = float(np.finfo(np.float32).eps)
MBLK = 256
P6PP = True
NBLK = CAP // MBLK

DT_SZ = {F32: 4, BF16: 2, U8: 1, U32: 4}


class Plan:
    ENGINES = ("sync", "gpsimd", "vector", "scalar", "tensor")

    def __init__(self):
        self.q = {e: [] for e in self.ENGINES}
        self.sems = {}
        self.cnt = {}
        self.gates = {}

    def sem(self, *names):
        for n in names:
            self.cnt[n] = 0

    def wait(self, eng, sem, val=None):
        v = self.cnt[sem] if val is None else val
        if v <= 0:
            return
        self.q[eng].append(lambda e, s=sem, vv=v: e.wait_ge(self.sems[s], vv))

    def op(self, eng, fn, inc=None, n=1):
        if inc is not None:
            self.cnt[inc] += n
            self.q[eng].append(
                lambda e, f=fn, s=inc, k=n: f(e).then_inc(self.sems[s], k))
            return self.cnt[inc]
        self.q[eng].append(fn)
        return None

    DMA_SEMS = ("sld", "sst", "gld", "xns", "scs", "xis", "sts", "zst",
                "wf1", "wf2", "wo", "ini", "fcb", "gbl", "dbs", "outs",
                "dsp", "yst", "sx0", "sx1", "cg0", "cg1", "cg2", "cg3", "dzr", "ab0", "ab1",
                "lx0", "lx1", "mz0", "mz1", "xf0", "xf1",
                "wi0", "wi1", "wi2", "wi3")

    def gate(self, eng, res):
        g = self.gates.get(res)
        if g is not None:
            if g[0] in self.DMA_SEMS:
                self.wait(eng, g[0])
            else:
                self.wait(eng, g[0], g[1])

    def set_gate(self, res, sem, val):
        self.gates[res] = (sem, val)


class SBAlloc:

    def __init__(self, arena_ap, total):
        self.a = arena_ap
        self.total = total
        self.items = []

    def alloc(self, name, shape, dtype, phases):
        n = 1
        for s in shape[1:]:
            n *= s
        nbytes = n * DT_SZ[dtype]
        ph = frozenset(phases)
        off = 0
        while True:
            off = (off + 63) // 64 * 64
            conflict = None
            for (s0, e0, p0, _nm) in self.items:
                if s0 < off + nbytes and off < e0 and (ph & p0 or "C" in ph or "C" in p0):
                    conflict = e0
            if conflict is None:
                break
            off = conflict
        assert off + nbytes <= self.total, (name, off, nbytes)
        self.items.append((off, off + nbytes, ph, name))
        v = self.a[:, off:off + nbytes].bitcast(dtype)
        if len(shape) == 2:
            if shape[0] != 128:
                v = v[0:shape[0], :]
        elif len(shape) == 3:
            v = v.rearrange("p (a b) -> p a b", a=shape[1])
            if shape[0] != 128:
                v = v[0:shape[0], :, :]
        else:
            raise ValueError(shape)
        return v


DEBUG = False
DEBUG_INLINE = True


def build():
    nc = bass.Bass("TRN2", target_bir_lowering=False)

    x_in = nc.dram_tensor("x", [TOK, D], F32, kind="ExternalInput").ap()
    w_in = nc.dram_tensor("w_in", [D, 2 * DI], F32, kind="ExternalInput").ap()
    wabc = nc.dram_tensor("wabc", [DI, 80], F32, kind="ExternalInput").ap()
    w_out = nc.dram_tensor("w_out", [DI, D], F32, kind="ExternalInput").ap()
    wg_in = nc.dram_tensor("wg", [D, E], F32, kind="ExternalInput").ap()
    normw_in = nc.dram_tensor("norm_w", [128, 16], F32, kind="ExternalInput").ap()
    fc1t_in = nc.dram_tensor("fc1t", [D, H], F32, kind="ExternalInput").ap()
    fc1b_in = nc.dram_tensor("fc1b", [128, 32], F32, kind="ExternalInput").ap()
    fc2_in = nc.dram_tensor("fc2", [H, D], F32, kind="ExternalInput").ap()
    fc2b_in = nc.dram_tensor("fc2b", [1, D], F32, kind="ExternalInput").ap()
    gamma_in = nc.dram_tensor("gamma", [1, D], F32, kind="ExternalInput").ap()
    beta_in = nc.dram_tensor("beta", [1, D], F32, kind="ExternalInput").ap()
    consts_in = nc.dram_tensor("consts", [1, 8], F32, kind="ExternalInput").ap()
    o_out = nc.dram_tensor("out", [TOK, D], F32, kind="ExternalOutput").ap()
    if DEBUG:
        o_xnb = nc.dram_tensor("o_xnb", [TOK, D], BF16, kind="ExternalOutput").ap()
        o_scl = nc.dram_tensor("o_scl", [NTOK, E], F32, kind="ExternalOutput").ap()
        o_xin = nc.dram_tensor("o_xin", [DI, TOK], BF16, kind="ExternalOutput").ap()
        o_st = nc.dram_tensor("o_st", [2 * NS, DI], F32, kind="ExternalOutput").ap()
        o_moe = nc.dram_tensor("o_moe", [CAP, D], BF16, kind="ExternalOutput").ap()
        o_ssm = nc.dram_tensor("o_ssm", [TOK, D], BF16, kind="ExternalOutput").ap()
        o_rt = nc.dram_tensor("o_rt", [128, 16 * 128], F32, kind="ExternalOutput").ap()
        o_m2 = nc.dram_tensor("o_m2", [128, (NCH // 2) * 64], BF16, kind="ExternalOutput").ap()
        o_ctp = nc.dram_tensor("o_ctp", [16, NCH * 64], F32, kind="ExternalOutput").ap()
        o_bd = nc.dram_tensor("o_bd", [128, (NCH // 2) * 16], BF16, kind="ExternalOutput").ap()
        o_pm = nc.dram_tensor("o_pm", [16, NCH], F32, kind="ExternalOutput").ap()
        o_se = nc.dram_tensor("o_se", [16, TOK], F32, kind="ExternalOutput").ap()
        o_so = nc.dram_tensor("o_so", [128, NCH * 1024], BF16, kind="ExternalOutput").ap()

    d_xnb = nc.dram_tensor("d_xnb", [TOK, D], BF16).ap()
    d_xnlo = nc.dram_tensor("d_xnlo", [TOK, D], BF16).ap()
    d_xn_all = nc.dram_tensor("d_xn_all", [NTOK, D], BF16).ap()
    d_scl = nc.dram_tensor("d_scl", [TOK, E], F32).ap()
    d_sc_all = nc.dram_tensor("d_sc_all", [NTOK, E], F32).ap()
    d_xin = nc.dram_tensor("d_xin", [DI, TOK], BF16).ap()
    d_state = nc.dram_tensor("d_state", [NS, DI], F32).ap()
    d_state_ag = nc.dram_tensor("d_state_ag", [2 * NS, DI], F32).ap()
    d_disp = nc.dram_tensor("d_disp", [CAP, D], BF16).ap()
    d_y = nc.dram_tensor("d_y", [CAP, D], BF16).ap()
    d_y_all = nc.dram_tensor("d_y_all", [E * CAP, D], BF16).ap()

    P = Plan()
    global _LAST_PLAN
    _LAST_PLAN = P
    P.sem("sld", "sst", "gld", "cc", "pe", "dve", "acts", "gpc", "outs")
    P.sem("xns", "scs", "xis", "sts", "zst", "wf1", "wf2", "wo", "ini", "fcb", "gbl")
    P.sem("dsp", "yst", "sx0", "sx1", "cg0", "cg1", "cg2", "cg3", "dzr", "ab0", "ab1")
    P.sem("lx0", "lx1", "mz0", "mz1", "xf0", "xf1")
    P.sem("wi0", "wi1", "wi2", "wi3")
    P.sem("xt0", "xt1", "xt2", "xt3")
    xts = ["xt0", "xt1", "xt2", "xt3"]
    if DEBUG:
        P.sem("dbg")
        P.sem("dbs")
    wis = ["wi0", "wi1", "wi2", "wi3"]
    lxs = ["lx0", "lx1"]

    with ExitStack() as es:
        EC = es.enter_context
        SBT = 204 * 1024
        arena_t = EC(nc.sbuf_tensor("arena", [128, SBT], U8))
        A = SBAlloc(arena_t.ap(), SBT)
        ALLPH = ["C"]

        def phs(*p):
            return ALLPH if DEBUG else list(p)

        icol = A.alloc("icol", [128, 1], F32, ["C"])
        iota8 = A.alloc("iota8", [128, 8], F32, ["C"])
        onesb = A.alloc("onesb", [128, 1], BF16, ["C"])
        onescf = A.alloc("onescf", [128, 1], F32, ["C"])
        epsln = A.alloc("epsln", [128, 1], F32, ["C"])
        epsrms = A.alloc("epsrms", [1, 1], F32, ["C"])
        ones_rf = A.alloc("ones_rf", [1, 128], F32, ["C"])
        constr = A.alloc("constr", [1, 8], F32, ["C"])
        myev = A.alloc("myev", [128, 1], F32, ["C"])
        isoddv = A.alloc("isoddv", [128, 1], F32, ["C"])
        wg_hi = A.alloc("wg_hi", [128, 8, 8], BF16, ["C"])
        wg_lo = A.alloc("wg_lo", [128, 8, 8], BF16, ["C"])
        normw = A.alloc("normw", [128, 16], F32, ["C"])
        fc1b_sb = A.alloc("fc1b_sb", [128, 32], F32, ["C"])
        fc2b_row = A.alloc("fc2b_row", [1, 1024], F32, ["MO"])
        gam_row = A.alloc("gam_row", [1, 1024], F32, ["P7"])
        bet_row = A.alloc("bet_row", [1, 1024], F32, ["P7"])
        identb = A.alloc("identb", [128, 128], BF16, ["C"])
        identf = A.alloc("identf", [128, 128], F32, ["C"])
        tril_b = A.alloc("tril_b", [128, 128], BF16, ["C"])
        irow128 = A.alloc("irow128", [128, 128], F32, ["C"])
        zeros128 = A.alloc("zeros128", [128, 128], F32, ["C"])
        wabc_sb = A.alloc("wabc_sb", [128, 16, 80], BF16, ["C"])
        m2m_all = A.alloc("m2m_all", [128, (NCH // 2) * 64], BF16, ["P0", "SC", "P3", "MO", "P6"])
        ctp_all = A.alloc("ctp_all", [16, NCH * 64], F32, ["P0", "SC", "P3", "MO", "P6"])
        bd_all = A.alloc("bd_all", [128, (NCH // 2) * 16], BF16, ["P0", "SC", "P3", "MO", "P6"])
        pmat = A.alloc("pmat", [16, NCH], F32, ["P0", "SC", "P3", "MO", "P6"])
        stateA = A.alloc("stateA", [16, DI], F32, ["P3"])
        stateB = A.alloc("stateB", [16, DI], F32, ["P3"])
        st6A = A.alloc("st6A", [16, DI], F32, ["P6"])
        st6B = A.alloc("st6B", [16, DI], F32, ["P6"])
        zrow = A.alloc("zrow", [128, 2048], U8, ["C"])
        scidx0 = A.alloc("scidx0", [128, 128], U32, ["P3", "MO"])
        scidx1 = A.alloc("scidx1", [128, 128], U32, ["P3", "MO"])
        ohmy = A.alloc("ohmy", [128, 16], F32, ["P3"])
        mycol = A.alloc("mycol", [128, 1], F32, ["P3"])
        cidx_loc = A.alloc("cidx_loc", [128, 32], U32, ["P3", "MO", "P6", "P7"])
        g_loc = A.alloc("g_loc", [128, 32], F32, ["P3", "MO", "P6", "P7"])
        sxn = [A.alloc(f"sxn{i}", [128, 4, 1024], BF16, ["P3", "MO"]) for i in range(2)]
        ybuf = [A.alloc(f"ybuf{i}", [128, 1024], BF16, ["P7"]) for i in range(4)]
        fc2b_bc = A.alloc("fc2b_bc", [128, 1024], F32, ["MO"])
        gb_bc = A.alloc("gb_bc", [128, 2048], F32, ["P7"])
        init_st = A.alloc("init_st", [16, DI], F32, ["P6"])
        ct_t = A.alloc("ct_t", [16, 64], BF16, ["P3"])
        bs_t = A.alloc("bs_t", [16, 64], BF16, ["P3"])
        bdb = A.alloc("bdb", [16, 64], BF16, ["P3"])
        f_t = A.alloc("f_t", [16, 64], F32, ["P3"])
        sr_m = A.alloc("sr_m", [16, 1], F32, ["P3"])
        sr_p = A.alloc("sr_p", [16, 1], F32, ["P3"])
        sr_e = A.alloc("sr_e", [16, 1], F32, ["P3"])
        lnv = [A.alloc(f"lnv{i}", [128, 1], F32, ["P0", "P7"]) for i in range(6)]
        sc_sb = A.alloc("sc_sb", [128, 8], F32, ["P0", "SC"])

        hi_fm = A.alloc("hi_fm", [128, 8, TOK], BF16, ["P0", "SC", "P3"])
        lo_fm = A.alloc("lo_fm", [128, 8, TOK], BF16, ["P0", "SC"])
        xring = [A.alloc(f"xring{i}", [128, 1024], F32, ["P0", "P7"]) for i in range(2)]
        xnf = A.alloc("xnf", [128, 1024], F32, ["P0", "P7"])
        hi_t = A.alloc("hi_t", [128, 1024], BF16, ["P0"])
        lo_t = A.alloc("lo_t", [128, 1024], BF16, ["P0"])
        sqdump = A.alloc("sqdump", [128, 1024], F32, ["P0", "P7"])

        winr = [A.alloc(f"winr{i}", [128, 8, 128], BF16, ["P3"]) for i in range(4)]
        xinr = [A.alloc(f"xinr{i}", [128, 512], BF16, ["P3"]) for i in range(4)]
        abstr = [A.alloc(f"abstr{i}", [128, TOK], BF16, ["P3"]) for i in range(2)]
        ga_t = A.alloc("ga_t", [128, 512], BF16, ["P3"])
        gb_t = A.alloc("gb_t", [128, 512], BF16, ["P3"])
        a_raw = A.alloc("a_raw", [16, TOK], F32, ["P3"])
        b_raw = A.alloc("b_raw", [16, TOK], F32, ["P3"])
        c_raw = A.alloc("c_raw", [16, TOK], F32, ["P3"])
        se_t = A.alloc("se_t", [16, TOK], F32, ["P3"])
        xtm3 = [A.alloc(f"xtm3_{i}", [128, DI], BF16, ["P3"]) for i in range(2)]

        r_S = A.alloc("r_S", [128, 128, 8], F32, ["P3"])
        r_e = A.alloc("r_e", [128, 128, 8], F32, ["P3"])
        r_oh = A.alloc("r_oh", [128, 16, 128], BF16, ["P3"])
        r_scan = A.alloc("r_scan", [128, 16, 128], F32, ["P3"])
        r_t8 = A.alloc("r_t8", [128, 128, 8], F32, ["P3"])
        r_m = A.alloc("r_m", [128, 128], F32, ["P3"])
        r_ss = A.alloc("r_ss", [128, 128], F32, ["P3"])
        r_g1 = A.alloc("r_g1", [128, 128], F32, ["P3"])
        r_g2 = A.alloc("r_g2", [128, 128], F32, ["P3"])
        r_i1 = A.alloc("r_i1", [128, 128], F32, ["P3"])
        r_i2 = A.alloc("r_i2", [128, 128], F32, ["P3"])
        r_k0 = A.alloc("r_k0", [128, 128], F32, ["P3"])
        r_geJ = A.alloc("r_geJ", [128, 128], F32, ["P3"])
        dbg_rt = A.alloc("dbg_rt", [128, 16, 128], F32, ["P3"]) if DEBUG else None
        r_tot = A.alloc("r_tot", [128, 16], F32, ["P3"])
        r_totb = A.alloc("r_totb", [128, 16], BF16, ["P3"])
        r_base = A.alloc("r_base", [128, 16], F32, ["P3"])
        r_off = A.alloc("r_off", [1, 16], F32, ["P3"])
        r_adj = A.alloc("r_adj", [128, 16], F32, ["P3"])

        wfc1 = A.alloc("wfc1", [128, 8, H], BF16, ["MO"])
        wfc2 = A.alloc("wfc2", [128, 32, D], BF16, ["MO"])
        h_fm = A.alloc("h_fm", [128, 32, MBLK], BF16, ["MO"])
        xfm = [A.alloc(f"xfm{i}", [128, 8, MBLK], BF16, ["MO"]) for i in range(2)]
        y_sb = [A.alloc(f"y_sb{i}", [128, 1024], BF16, ["MO"]) for i in range(2)]

        woutsb = A.alloc("woutsb", [128, 16, D], BF16, ["P6"])
        ssm_tm = A.alloc("ssm_tm", [128, NT, D], BF16, ["P6", "P7"])
        out_nrm = A.alloc("out_nrm", [128, 16, 128], BF16, ["P6"])
        sq_sb = A.alloc("sq_sb", [128, 1024], F32, ["P6"])
        rstd_bc = A.alloc("rstd_bc", [128, 64], F32, ["P6"])
        rms_row = A.alloc("rms_row", [1, 64], F32, ["P6"])
        xtm6 = [A.alloc(f"xtm6_{i}", [128, DI], BF16, ["P6"]) for i in range(2)]
        dbg_so = A.alloc("dbg_so", [128, 1024], BF16, ["P6"]) if DEBUG else None

        moe_t = A.alloc("moe_t", [128, 1024], F32, ["P7"])
        res_t = A.alloc("res_t", [128, 1024], F32, ["P7"])
        ffn_t = A.alloc("ffn_t", [128, 1024], F32, ["P7"])
        fin_t = A.alloc("fin_t", [128, 1024], F32, ["P7"])

        top = max(e for (_s, e, _p, _n) in A.items)
        assert top <= SBT, top


        def bd_sl(k):
            return bd_all[(k % 2) * 64:(k % 2) * 64 + 64,
                          (k // 2) * 16:(k // 2) * 16 + 16]

        def m2_sl(k):
            return m2m_all[(k % 2) * 64:(k % 2) * 64 + 64,
                           (k // 2) * 64:(k // 2) * 64 + 64]
        psA = EC(nc.psum_tensor("psA", [128, 512], F32))[:, :]
        psB = EC(nc.psum_tensor("psB", [128, 512], F32))[:, :]
        psY0 = EC(nc.psum_tensor("psY0", [128, 512], F32))[:, :]
        psY1 = EC(nc.psum_tensor("psY1", [128, 512], F32))[:, :]
        psM = EC(nc.psum_tensor("psM", [128, 512], F32))[:, :]
        psO = EC(nc.psum_tensor("psO", [128, 1024], F32))[:, :]
        psT = EC(nc.psum_tensor("psT", [128, 512], F32))[:, :]

        P.op("gpsimd", lambda g: g.iota(icol, pattern=[[0, 1]], base=0,
             channel_multiplier=1, allow_small_or_imprecise_dtypes=True))
        P.op("gpsimd", lambda g: g.iota(iota8, pattern=[[1, 8]], base=0,
             channel_multiplier=0, allow_small_or_imprecise_dtypes=True))
        P.op("gpsimd", lambda g: g.iota(irow128, pattern=[[1, 128]], base=0,
             channel_multiplier=0, allow_small_or_imprecise_dtypes=True))
        P.op("gpsimd", lambda g: g.memset(onesb, 1.0))
        P.op("gpsimd", lambda g: g.memset(onescf, 1.0))
        P.op("gpsimd", lambda g: g.memset(epsln, LN_EPS))
        P.op("gpsimd", lambda g: g.memset(epsrms, RMS_EPS))
        P.op("gpsimd", lambda g: g.memset(zrow.bitcast(BF16), 0.0), inc="gpc")
        P.op("gpsimd", lambda g: g.memset(ones_rf, 1.0))
        P.op("gpsimd", lambda g: g.memset(zeros128, 0.0), inc="gpc")
        P.op("sync", lambda s: s.dma_start(out=constr, in_=consts_in[:, :]), inc="sld", n=16)
        P.op("sync", lambda s: s.dma_start(out=normw, in_=normw_in[:, :]), inc="sld", n=16)
        P.op("sync", lambda s: s.dma_start(out=fc1b_sb, in_=fc1b_in[:, :]), inc="sld", n=16)
        P.op("gpsimd", lambda g: g.dma_start(
            out=wabc_sb, in_=wabc.rearrange("(dc p) w -> p dc w", p=128)), inc="gld", n=16)
        wgf_ld = P.op("gpsimd", lambda g: g.dma_start(
            out=wg_hi, in_=wg_in.rearrange("(dc p) e -> p dc e", p=128)), inc="gld", n=16)
        wg_f = A.alloc("wg_f", [128, 8, 8], F32, ["P0"])
        P.op("sync", lambda s: s.dma_start(
            out=wg_f, in_=wg_in.rearrange("(dc p) e -> p dc e", p=128)), inc="sld", n=16)
        SMALL_LD = P.cnt["sld"]
        P.wait("vector", "gpc")
        P.wait("vector", "sld", SMALL_LD)
        P.wait("vector", "gld")
        P.op("vector", lambda v: v.tensor_scalar(identb, irow128, icol[:, 0:1], None, ALU.is_equal))
        P.op("vector", lambda v: v.tensor_scalar(identf, irow128, icol[:, 0:1], None, ALU.is_equal))
        P.op("vector", lambda v: v.tensor_scalar(tril_b, irow128, icol[:, 0:1], None, ALU.is_ge), inc="dve")
        P.op("vector", lambda v: v.tensor_tensor(wg_lo, wg_f, wg_hi, ALU.subtract), inc="dve")
        WG_DONE = P.cnt["dve"]
        P.wait("tensor", "sld", SMALL_LD)
        P.wait("tensor", "gpc")
        pv = P.op("tensor", lambda t: t.matmul(psT[:, 0:8], ones_rf, constr, start=True, stop=True), inc="pe")
        P.wait("vector", "pe", pv)
        P.op("vector", lambda v: v.tensor_copy(isoddv, psT[:, 0:1]))
        P.op("vector", lambda v: v.tensor_copy(myev, psT[:, 1:2]), inc="dve")
        P.set_gate("psT", "dve", P.cnt["dve"])

        P.wait("scalar", "gpc")
        for zb in range(CAP // 128):
            P.op("scalar", lambda s, zb=zb: s.dma_start(
                out=d_disp[zb * 128:(zb + 1) * 128, :], in_=zrow.bitcast(BF16)),
                inc="dzr", n=16)

        ln_done = {}
        for i in range(NT):
            sl = lxs[i % 2]
            if i >= 2:
                P.wait("sync", "dve", ln_done[i - 2])
            P.op("sync", lambda s, i=i: s.dma_start(
                out=xring[i % 2], in_=x_in[i * 128:(i + 1) * 128, :]), inc=sl, n=16)
            P.wait("scalar", sl)
            P.gate("scalar", "lnv0")
            P.op("scalar", lambda s, b=xring[i % 2]: s.activation(
                sqdump, b, AF.Square, accum_out=lnv[0]), inc="acts")
            ac_sq = P.cnt["acts"]
            P.wait("vector", sl)
            P.op("vector", lambda v, b=xring[i % 2]: v.tensor_reduce(
                lnv[1], b, AX.X, ALU.add), inc="dve")
            P.wait("vector", "dve")
            P.op("vector", lambda v: v.tensor_scalar(lnv[2], lnv[1], 1.0 / D, None, ALU.mult), inc="dve")
            P.wait("vector", "dve")
            P.op("vector", lambda v: v.tensor_tensor(lnv[3], lnv[2], lnv[2], ALU.mult), inc="dve")
            P.wait("vector", "acts", ac_sq)
            P.wait("vector", "dve")
            dv = P.op("vector", lambda v: v.scalar_tensor_tensor(
                lnv[4], lnv[0], 1.0 / D, lnv[3], ALU.mult, ALU.subtract), inc="dve")
            P.set_gate("lnv0", "dve", dv)
            P.wait("scalar", "dve", dv)
            P.op("scalar", lambda s: s.activation(lnv[5], lnv[4], AF.Sqrt, bias=epsln[:, 0:1]), inc="acts")
            P.wait("vector", "acts")
            P.op("vector", lambda v: v.reciprocal(lnv[4], lnv[5]), inc="dve")
            P.wait("vector", "dve")
            P.op("vector", lambda v, b=xring[i % 2]: v.tensor_scalar(
                xnf, b, lnv[2][:, 0:1], lnv[4][:, 0:1], ALU.subtract, ALU.mult), inc="dve")
            P.wait("vector", "dve")
            P.gate("vector", "hilo")
            P.op("vector", lambda v: v.tensor_copy(hi_t, xnf), inc="dve")
            P.wait("vector", "dve")
            dv = P.op("vector", lambda v: v.tensor_tensor(lo_t, xnf, hi_t, ALU.subtract), inc="dve")
            ln_done[i] = dv
            P.wait("sync", "dve", dv)
            P.op("sync", lambda s, i=i: s.dma_start(
                out=d_xnb[i * 128:(i + 1) * 128, :], in_=hi_t), inc="xns", n=16)
            sv = P.op("sync", lambda s, i=i: s.dma_start(
                out=d_xnlo[i * 128:(i + 1) * 128, :], in_=lo_t), inc="xns", n=16)
            P.set_gate("hilo", "xns", sv)

        P.wait("sync", "xns")
        for dc in range(8):
            P.op("sync", lambda s, dc=dc: s.dma_start(
                out=hi_fm[:, dc, :], in_=d_xnb[:, dc * 128:(dc + 1) * 128],
                transpose=True), inc="sld", n=16)
            P.op("sync", lambda s, dc=dc: s.dma_start(
                out=lo_fm[:, dc, :], in_=d_xnlo[:, dc * 128:(dc + 1) * 128],
                transpose=True), inc="sld", n=16)
        FM_LD = P.cnt["sld"]

        P.wait("tensor", "sld", FM_LD)
        P.wait("tensor", "dve", WG_DONE)
        for i in range(NT):
            P.gate("tensor", "psM")
            nmm = 0
            for (af, bf) in ((hi_fm, wg_hi), (hi_fm, wg_lo), (lo_fm, wg_hi)):
                for dc in range(8):
                    nmm += 1
                    pv = P.op("tensor", lambda t, af=af, bf=bf, dc=dc, i=i,
                              st=(nmm == 1), sp=(nmm == 24):
                              t.matmul(psM[:, 0:8], af[:, dc, i * 128:(i + 1) * 128],
                                       bf[:, dc, :], start=st, stop=sp),
                              inc="pe" if nmm == 24 else None)
            P.wait("vector", "pe", pv)
            P.gate("vector", "sc_sb")
            dv = P.op("vector", lambda v: v.tensor_copy(sc_sb, psM[:, 0:8]), inc="dve")
            P.set_gate("psM", "dve", dv)
            P.wait("sync", "dve", dv)
            sv = P.op("sync", lambda s, i=i: s.dma_start(
                out=d_scl[i * 128:(i + 1) * 128, :], in_=sc_sb), inc="scs", n=16)
            P.set_gate("sc_sb", "scs", sv)

        SCORES_PE = P.cnt["pe"]

        P.wait("gpsimd", "xns")
        P.wait("gpsimd", "scs")
        P.op("gpsimd", lambda g: g.collective_compute(
            "AllGather", ALU.bypass, replica_groups=[list(range(8))],
            ins=[d_xnb], outs=[d_xn_all]), inc="cc")
        P.op("gpsimd", lambda g: g.collective_compute(
            "AllGather", ALU.bypass, replica_groups=[list(range(8))],
            ins=[d_scl], outs=[d_sc_all]), inc="cc")
        CC_AG = P.cnt["cc"]
        P.wait("sync", "cc", CC_AG)
        P.op("sync", lambda s: s.dma_start(
            out=r_S.rearrange("p a b -> p (a b)"),
            in_=d_sc_all.rearrange("(p n) e -> p (n e)", p=128)), inc="sld", n=16)
        SC_LD = P.cnt["sld"]
        i8b = iota8.rearrange("p e -> p () e").broadcast_to((128, 128, 8))

        def bc1(t2):
            return t2.rearrange("p a -> p a ()").broadcast_to((128, 128, 8))
        P.wait("vector", "sld", SC_LD)
        P.wait("vector", "pe", SCORES_PE)
        P.op("vector", lambda v: v.tensor_reduce(r_m, r_S, AX.X, ALU.max, negate=True), inc="dve")
        P.wait("vector", "dve")
        P.op("vector", lambda v: v.tensor_tensor(r_e, r_S, bc1(r_m), ALU.add), inc="dve")
        P.wait("scalar", "dve")
        P.op("scalar", lambda s: s.activation(
            r_e.rearrange("p a b -> p (a b)"), r_e.rearrange("p a b -> p (a b)"),
            AF.Exp), inc="acts")
        P.wait("vector", "acts")
        P.op("vector", lambda v: v.tensor_reduce(r_ss, r_e, AX.X, ALU.add), inc="dve")
        P.wait("vector", "dve")
        P.op("vector", lambda v: v.reciprocal(r_m, r_ss), inc="dve")
        P.wait("vector", "dve")
        P.op("vector", lambda v: v.tensor_tensor(r_e, r_e, bc1(r_m), ALU.mult), inc="dve")
        P.wait("vector", "dve")
        P.op("vector", lambda v: v.tensor_reduce(r_g1, r_e, AX.X, ALU.max), inc="dve")
        P.wait("vector", "dve")
        P.op("vector", lambda v: v.tensor_tensor(r_t8, r_e, bc1(r_g1), ALU.is_lt), inc="dve")
        P.wait("vector", "dve")
        P.op("vector", lambda v: v.scalar_tensor_tensor(
            r_t8, r_t8, 100.0, i8b, ALU.mult, ALU.add), inc="dve")
        P.wait("vector", "dve")
        P.op("vector", lambda v: v.tensor_reduce(r_i1, r_t8, AX.X, ALU.min), inc="dve")
        P.wait("vector", "dve")
        oh_r0 = r_oh[:, 0:8, :].rearrange("p c j -> p j c")
        oh_r1 = r_oh[:, 8:16, :].rearrange("p c j -> p j c")
        P.op("vector", lambda v: v.tensor_tensor(oh_r0, i8b, bc1(r_i1), ALU.is_equal), inc="dve")
        P.wait("vector", "dve")
        P.op("vector", lambda v: v.scalar_tensor_tensor(
            r_t8, oh_r0, -2.0, r_e, ALU.mult, ALU.add), inc="dve")
        P.wait("vector", "dve")
        P.op("vector", lambda v: v.tensor_reduce(r_g2, r_t8, AX.X, ALU.max), inc="dve")
        P.wait("vector", "dve")
        P.op("vector", lambda v: v.tensor_tensor(r_t8, r_t8, bc1(r_g2), ALU.is_lt), inc="dve")
        P.wait("vector", "dve")
        P.op("vector", lambda v: v.scalar_tensor_tensor(
            r_t8, r_t8, 100.0, i8b, ALU.mult, ALU.add), inc="dve")
        P.wait("vector", "dve")
        P.op("vector", lambda v: v.tensor_reduce(r_i2, r_t8, AX.X, ALU.min), inc="dve")
        P.wait("vector", "dve")
        P.op("vector", lambda v: v.tensor_tensor(oh_r1, i8b, bc1(r_i2), ALU.is_equal), inc="dve")
        P.wait("vector", "dve")
        P.op("vector", lambda v: v.tensor_tensor(r_ss, r_g1, r_g2, ALU.add), inc="dve")
        P.wait("vector", "dve")
        P.op("vector", lambda v: v.tensor_scalar(r_ss, r_ss, EPS_TOP, None, ALU.max), inc="dve")
        P.wait("vector", "dve")
        P.op("vector", lambda v: v.reciprocal(r_m, r_ss), inc="dve")
        P.wait("vector", "dve")
        P.op("vector", lambda v: v.tensor_tensor(r_g1, r_g1, r_m, ALU.mult))
        P.op("vector", lambda v: v.tensor_tensor(r_g2, r_g2, r_m, ALU.mult), inc="dve")
        if DEBUG and DEBUG_INLINE:
            P.wait("vector", "dve")
            P.op("vector", lambda v: v.tensor_copy(dbg_rt[:, 0, :], r_g1))
            P.op("vector", lambda v: v.tensor_copy(dbg_rt[:, 1, :], r_g2))
            P.op("vector", lambda v: v.tensor_copy(dbg_rt[:, 2, :], r_i1))
            P.op("vector", lambda v: v.tensor_copy(dbg_rt[:, 3, :], r_i2), inc="dve")
        P.wait("vector", "dve")
        for ch in range(16):
            P.op("vector", lambda v, ch=ch: v.tensor_tensor_scan(
                r_scan[:, ch, :], zeros128[:, 0:1].broadcast_to((128, 128)),
                r_oh[:, ch, :], 0.0, ALU.add, ALU.add),
                inc="dve" if ch == 15 else None)
        P.wait("vector", "dve")
        P.op("vector", lambda v: v.tensor_copy(
            r_tot, r_scan[:, :, 127:128].rearrange("p c a -> p (c a)")), inc="dve")
        P.wait("vector", "dve")
        P.op("vector", lambda v: v.tensor_copy(r_totb, r_tot), inc="dve")
        P.wait("tensor", "dve")
        P.gate("tensor", "psT")
        pv = P.op("tensor", lambda t: t.matmul(psT[:, 0:16], tril_b, r_totb,
                                               start=True, stop=True), inc="pe")
        P.wait("vector", "pe", pv)
        P.op("vector", lambda v: v.tensor_tensor(r_base, psT[:, 0:16], r_tot, ALU.subtract), inc="dve")
        pv = P.op("tensor", lambda t: t.matmul(psT[0:16, 16:17], r_totb, onesb,
                                               start=True, stop=True), inc="pe")
        P.wait("vector", "pe", pv)
        P.wait("vector", "dve")
        P.op("vector", lambda v: v.tensor_copy(r_tot[0:16, 0:1], psT[0:16, 16:17]), inc="dve")
        P.wait("tensor", "dve")
        pv = P.op("tensor", lambda t: t.transpose(psT[0:1, 32:48], r_tot[0:16, 0:1],
                                                  identf[0:16, 0:16]), inc="pe")
        P.wait("vector", "pe", pv)
        P.op("vector", lambda v: v.memset(r_off, 0.0), inc="dve")
        P.wait("vector", "dve")
        P.op("vector", lambda v: v.tensor_copy(r_off[:, 8:16], psT[0:1, 32:40]), inc="dve")
        P.wait("tensor", "dve")
        pv = P.op("tensor", lambda t: t.matmul(psT[:, 48:64], ones_rf, r_off,
                                               start=True, stop=True), inc="pe")
        P.wait("vector", "pe", pv)
        P.op("vector", lambda v: v.scalar_tensor_tensor(
            r_adj, r_base, -1.0, psT[:, 48:64], ALU.add, ALU.add), inc="dve")
        P.wait("vector", "dve")
        P.op("vector", lambda v: v.tensor_tensor(
            r_scan, r_scan,
            r_adj.rearrange("p c -> p c ()").broadcast_to((128, 16, 128)),
            ALU.add), inc="dve")
        P.wait("vector", "dve")
        P.op("vector", lambda v: v.tensor_tensor(
            r_t8, r_scan[:, 0:8, :].rearrange("p c j -> p j c"), oh_r0, ALU.mult), inc="dve")
        P.wait("vector", "dve")
        P.op("vector", lambda v: v.tensor_reduce(r_ss, r_t8, AX.X, ALU.add), inc="dve")
        P.wait("vector", "dve")
        P.op("vector", lambda v: v.tensor_scalar(r_k0, r_ss, float(CAP), None, ALU.is_lt), inc="dve")
        if DEBUG and DEBUG_INLINE:
            P.op("vector", lambda v: v.tensor_copy(dbg_rt[:, 4, :], r_ss))
            P.op("vector", lambda v: v.tensor_copy(dbg_rt[:, 5, :], r_k0), inc="dve")
        P.wait("vector", "dve")
        P.op("vector", lambda v: v.tensor_tensor(r_g1, r_g1, r_k0, ALU.mult), inc="dve")
        P.op("vector", lambda v: v.tensor_tensor(
            r_t8, r_scan[:, 8:16, :].rearrange("p c j -> p j c"), oh_r1, ALU.mult), inc="dve")
        P.wait("vector", "dve")
        P.op("vector", lambda v: v.tensor_reduce(r_ss, r_t8, AX.X, ALU.add), inc="dve")
        P.wait("vector", "dve")
        P.op("vector", lambda v: v.tensor_scalar(r_k0, r_ss, float(CAP), None, ALU.is_lt), inc="dve")
        P.wait("vector", "dve")
        P.op("vector", lambda v: v.tensor_tensor(r_g2, r_g2, r_k0, ALU.mult), inc="dve")
        if DEBUG and DEBUG_INLINE:
            P.op("vector", lambda v: v.tensor_copy(dbg_rt[:, 6, :], r_ss))
            P.op("vector", lambda v: v.tensor_copy(dbg_rt[:, 8, 0:16], r_base), inc="dve")
        P.wait("vector", "dve")
        BIGIDX = 1.0e6
        for (rk, ohr, idr, sct) in ((0, oh_r0, r_i1, scidx0), (1, oh_r1, r_i2, scidx1)):
            sl = r_scan[:, 0:8, :] if rk == 0 else r_scan[:, 8:16, :]
            P.op("vector", lambda v, sl=sl, ohr=ohr: v.tensor_tensor(
                r_t8, sl.rearrange("p c j -> p j c"), ohr, ALU.mult), inc="dve")
            P.wait("vector", "dve")
            P.op("vector", lambda v: v.tensor_reduce(r_ss, r_t8, AX.X, ALU.add), inc="dve")
            P.wait("vector", "dve")
            P.op("vector", lambda v: v.tensor_scalar(r_k0, r_ss, float(CAP), None, ALU.is_lt), inc="dve")
            P.wait("vector", "dve")
            P.op("vector", lambda v, idr=idr: v.tensor_scalar(r_m, idr, myev[:, 0:1], None, ALU.is_equal), inc="dve")
            P.wait("vector", "dve")
            P.op("vector", lambda v: v.tensor_tensor(r_k0, r_k0, r_m, ALU.mult), inc="dve")
            P.wait("vector", "dve")
            P.op("vector", lambda v: v.tensor_scalar(r_ss, r_ss, -BIGIDX, None, ALU.add), inc="dve")
            P.wait("vector", "dve")
            P.op("vector", lambda v: v.scalar_tensor_tensor(
                r_ss, r_ss, BIGIDX, r_k0, ALU.bypass, ALU.mult) if False else
                v.tensor_tensor(r_ss, r_ss, r_k0, ALU.mult), inc="dve")
            P.wait("vector", "dve")
            P.op("vector", lambda v: v.tensor_scalar(r_ss, r_ss, BIGIDX, None, ALU.add), inc="dve")
            P.wait("vector", "dve")
            P.wait("tensor", "dve")
            P.gate("tensor", "psT")
            pv = P.op("tensor", lambda t: t.transpose(psT[:, 0:128], r_ss, identf), inc="pe")
            P.wait("vector", "pe", pv)
            dv = P.op("vector", lambda v, sct=sct: v.tensor_copy(sct, psT[:, 0:128]), inc="dve")
            P.set_gate("psT", "dve", dv)
            P.op("vector", lambda v, sl=sl, ohr=ohr: v.tensor_tensor(
                r_t8, sl.rearrange("p c j -> p j c"), ohr, ALU.mult), inc="dve")
            P.wait("vector", "dve")
            P.op("vector", lambda v: v.tensor_reduce(r_m, r_t8, AX.X, ALU.add), inc="dve")
            P.wait("vector", "dve")
            P.op("vector", lambda v: v.tensor_scalar(r_k0, r_m, float(CAP), None, ALU.is_lt), inc="dve")
            P.wait("vector", "dve")
            P.op("vector", lambda v, idr=idr: v.scalar_tensor_tensor(
                r_ss, idr, float(CAP), r_m, ALU.mult, ALU.add), inc="dve")
            P.wait("vector", "dve")
            dv = P.op("vector", lambda v: v.tensor_tensor(r_ss, r_ss, r_k0, ALU.mult), inc="dve")
            if rk == 0:
                P.op("vector", lambda v: v.tensor_scalar(mycol, myev, 16.0, None, ALU.mult), inc="dve")
                P.wait("vector", "dve")
                P.op("vector", lambda v: v.tensor_tensor(mycol, icol, mycol, ALU.subtract), inc="dve")
                P.wait("vector", "dve")
                for i in range(16):
                    P.op("vector", lambda v, i=i: v.tensor_scalar(
                        ohmy[:, i:i + 1], mycol, float(i), None, ALU.is_equal),
                        inc="dve" if i == 15 else None)
            P.wait("tensor", "dve")
            P.gate("tensor", "psT")
            pv = P.op("tensor", lambda t: t.matmul(psT[:, 0:16], r_ss, ohmy,
                                                   start=True, stop=True), inc="pe")
            P.wait("vector", "pe", pv)
            dv = P.op("vector", lambda v, rk=rk: v.tensor_copy(
                cidx_loc[:, rk * 16:(rk + 1) * 16], psT[:, 0:16]), inc="dve")
            gsrc = r_g1 if rk == 0 else r_g2
            pv = P.op("tensor", lambda t, gsrc=gsrc: t.matmul(psT[:, 16:32], gsrc, ohmy,
                                                              start=True, stop=True), inc="pe")
            P.wait("vector", "pe", pv)
            dv = P.op("vector", lambda v, rk=rk: v.tensor_copy(
                g_loc[:, rk * 16:(rk + 1) * 16], psT[:, 16:32]), inc="dve")
            P.set_gate("psT", "dve", dv)
        RT_DONE = P.cnt["dve"]

        XZ_PS = [(psA, psB, "psA", "psB"), (psY0, psY1, "psY0", "psY1")]
        ABC_PS = [(psO[0:80, 0:512], "psO"), (psO[0:80, 512:1024], "psO"),
                  (psT[0:80, :], "psT"), (psM[0:80, :], "psM")]
        for cp in range(16):
            for half in range(2):
                c = cp + 16 * half
                slot = (2 * cp + half) % 4
                P.gate("gpsimd", f"winr{slot}")
                P.op("gpsimd", lambda g, c=c, slot=slot: g.dma_start(
                    out=winr[slot], in_=w_in[:, c * 128:(c + 1) * 128].rearrange(
                        "(dc p) n -> p dc n", p=128)), inc=wis[slot], n=16)
            for tb in range(4):
                pa, pb, nma, nmb = XZ_PS[tb % 2]
                sla, slb = wis[(2 * cp) % 4], wis[(2 * cp + 1) % 4]
                P.wait("tensor", sla)
                P.wait("tensor", slb)
                P.gate("tensor", nma)
                P.gate("tensor", nmb)
                for dc in range(8):
                    P.op("tensor", lambda t, dc=dc, tb=tb, pa=pa, cp=cp:
                         t.matmul(pa, winr[(2 * cp) % 4][:, dc, :],
                                  hi_fm[:, dc, tb * 512:(tb + 1) * 512],
                                  start=(dc == 0), stop=(dc == 7)))
                for dc in range(8):
                    pv = P.op("tensor", lambda t, dc=dc, tb=tb, pb=pb, cp=cp:
                              t.matmul(pb, winr[(2 * cp + 1) % 4][:, dc, :],
                                       hi_fm[:, dc, tb * 512:(tb + 1) * 512],
                                       start=(dc == 0), stop=(dc == 7)),
                              inc="pe" if dc == 7 else None)
                P.set_gate(f"winr{(2 * cp) % 4}", "pe", pv)
                P.set_gate(f"winr{(2 * cp + 1) % 4}", "pe", pv)
                P.wait("scalar", "pe", pv)
                P.gate("scalar", "ga")
                P.op("scalar", lambda s, pa=pa: s.activation(ga_t, pa, AF.Silu), inc="acts")
                av = P.op("scalar", lambda s, pb=pb: s.activation(gb_t, pb, AF.Sigmoid), inc="acts")
                P.set_gate(nma, "acts", av)
                P.set_gate(nmb, "acts", av)
                xslot = (4 * cp + tb) % 4
                P.wait("vector", "acts", av)
                P.gate("vector", f"xinr{xslot}_st")
                dv = P.op("vector", lambda v, xslot=xslot: v.tensor_tensor(
                    xinr[xslot], ga_t, gb_t, ALU.mult), inc="dve")
                P.set_gate("ga", "dve", dv)
                P.wait("sync", "dve", dv)
                sv = P.op("sync", lambda s, cp=cp, tb=tb, xslot=xslot: s.dma_start(
                    out=d_xin[cp * 128:(cp + 1) * 128, tb * 512:(tb + 1) * 512],
                    in_=xinr[xslot]), inc="xis", n=16)
                P.set_gate(f"xinr{xslot}_st", "xis", sv)
        XZ_PE = P.cnt["pe"]
        P.wait("sync", "xis")
        abs_sems = ["sx0", "sx1"]
        for cp in range(16):
            absl = cp % 2
            P.gate("sync", f"abst{absl}")
            P.op("sync", lambda s, cp=cp, absl=absl: s.dma_start(
                out=abstr[absl], in_=d_xin[cp * 128:(cp + 1) * 128, :]),
                inc=["ab0", "ab1"][absl], n=16)
            P.wait("tensor", ["ab0", "ab1"][absl])
            for tb in range(4):
                abc_ps, abc_nm = ABC_PS[tb]
                if cp == 0:
                    P.gate("tensor", abc_nm)
                pv2 = P.op("tensor", lambda t, cp=cp, tb=tb, absl=absl, abc_ps=abc_ps:
                           t.matmul(abc_ps, wabc_sb[:, cp, :],
                                    abstr[absl][:, tb * 512:(tb + 1) * 512],
                                    start=(cp == 0), stop=(cp == 15)), inc="pe")
            P.set_gate(f"abst{absl}", "pe", pv2)
        ABC_PE = P.cnt["pe"]

        P.wait("vector", "pe", ABC_PE)
        for tb in range(4):
            ps = ABC_PS[tb][0]
            for (p0, dst) in ((0, a_raw), (32, b_raw), (64, c_raw)):
                dv = P.op("vector", lambda v, tb=tb, ps=ps, p0=p0, dst=dst:
                          v.tensor_copy(dst[:, tb * 512:(tb + 1) * 512],
                                        ps[p0:p0 + 16, :]), inc="dve")
        ABC_EV = P.cnt["dve"]
        for nm in ("psO", "psT", "psM"):
            P.set_gate(nm, "dve", ABC_EV)

        a_sl = a_raw
        b_sl = b_raw
        c_sl = c_raw
        P.wait("vector", "dve", ABC_EV)
        P.op("vector", lambda v: v.tensor_scalar(se_t, a_sl, -5.0, 0.0, ALU.max, ALU.min), inc="dve")
        P.wait("scalar", "dve")
        P.op("scalar", lambda s: s.activation(se_t, se_t, AF.Exp), inc="acts")
        P.wait("vector", "acts")
        P.op("vector", lambda v: v.tensor_tensor_scan(
            se_t, zeros128[0:16, 0:1].broadcast_to((16, TOK)), se_t, 0.0,
            ALU.add, ALU.add), inc="dve")

        if DEBUG and DEBUG_INLINE:
            P.wait("sync", "dve")
            P.op("sync", lambda s: s.dma_start(out=o_se[:, :], in_=se_t), inc="dbs", n=16)
        for k in range(NCH):
            s0, s1 = k * L, (k + 1) * L
            sec = se_t[:, s0:s1]
            P.wait("vector", "dve")
            P.gate("vector", "srefs")
            ap_mid = se_t[:, k * L + 31:k * L + 32]
            P.op("vector", lambda v, ap=ap_mid: v.tensor_copy(sr_m, ap))
            if k > 0:
                ap_prev = se_t[:, k * L - 1:k * L]
                P.op("vector", lambda v, ap=ap_prev: v.tensor_copy(sr_p, ap))
            else:
                P.op("vector", lambda v: v.memset(sr_p, 0.0))
            P.op("vector", lambda v, k=k: v.tensor_scalar(
                sr_e, se_t[:, (k + 1) * L - 1:(k + 1) * L], -1.0, None, ALU.mult), inc="dve")
            dv_ref = P.cnt["dve"]
            P.wait("scalar", "dve", dv_ref)
            P.gate("scalar", "f_t")
            P.op("scalar", lambda s, sec=sec: s.activation(
                f_t, sec, AF.Exp, bias=sr_m[:, 0:1], scale=-1.0), inc="acts")
            P.wait("vector", "acts")
            dv = P.op("vector", lambda v, k=k, sec=sec: v.tensor_tensor(
                ct_t, c_sl[:, k * L:(k + 1) * L], f_t, ALU.mult), inc="dve")
            P.wait("scalar", "dve", dv)
            P.op("scalar", lambda s, sec=sec: s.activation(
                f_t, sec, AF.Exp, bias=sr_p[:, 0:1], scale=-1.0), inc="acts")
            P.wait("vector", "acts")
            P.op("vector", lambda v, k=k: v.tensor_tensor(
                ctp_all[:, k * 64:(k + 1) * 64], c_sl[:, k * L:(k + 1) * L], f_t,
                ALU.mult), inc="dve")
            P.op("vector", lambda v: v.tensor_scalar(sr_m, sr_m, -1.0, None, ALU.mult), inc="dve")
            P.wait("scalar", "dve")
            P.op("scalar", lambda s, sec=sec: s.activation(
                f_t, sec, AF.Exp, bias=sr_m[:, 0:1], scale=1.0), inc="acts")
            P.wait("vector", "acts")
            P.op("vector", lambda v, k=k: v.tensor_tensor(
                bs_t, b_sl[:, k * L:(k + 1) * L], f_t, ALU.mult), inc="dve")
            P.wait("scalar", "dve")
            P.op("scalar", lambda s, sec=sec: s.activation(
                f_t, sec, AF.Exp, bias=sr_e[:, 0:1], scale=1.0), inc="acts")
            P.wait("vector", "acts")
            P.op("vector", lambda v, k=k: v.tensor_tensor(
                f_t, b_sl[:, k * L:(k + 1) * L], f_t, ALU.mult), inc="dve")
            P.wait("vector", "dve")
            dv = P.op("vector", lambda v: v.tensor_copy(bdb, f_t), inc="dve")
            P.set_gate("f_t", "dve", dv)
            P.wait("scalar", "dve", dv)
            P.op("scalar", lambda s, k=k: s.activation(
                pmat[:, k:k + 1], sr_p, AF.Exp, bias=sr_e[:, 0:1], scale=1.0), inc="acts")
            av_p = P.cnt["acts"]
            P.set_gate("srefs", "acts", av_p)
            if k == 0:
                P.gate("tensor", "psT")
            P.gate("tensor", "psTbd")
            P.wait("tensor", "dve", dv)
            tp_out = psT[0:64, 96:104].bitcast(BF16)
            pv = P.op("tensor", lambda t, tp_out=tp_out: t.transpose(
                tp_out, bdb, identb[0:16, 0:16]), inc="pe")
            P.wait("vector", "pe", pv)
            dvb = P.op("vector", lambda v, k=k, tp_out=tp_out: v.tensor_copy(
                bd_sl(k), tp_out), inc="dve")
            P.set_gate("psTbd", "dve", dvb)
            P.gate("tensor", "psTm2")
            P.wait("tensor", "dve")
            pv = P.op("tensor", lambda t: t.matmul(
                psT[0:64, 0:64], bs_t, ct_t, start=True, stop=True), inc="pe")
            P.wait("vector", "pe", pv)
            dvm = P.op("vector", lambda v, k=k: v.tensor_tensor(
                m2_sl(k), psT[0:64, 0:64],
                tril_b[0:64, 0:64], ALU.mult), inc="dve")
            P.set_gate("psTm2", "dve", dvm)
        P.set_gate("psT", "dve", P.cnt["dve"])

        P.wait("vector", "acts")
        P.op("vector", lambda v: v.memset(stateA, 0.0), inc="dve")
        st_cur, st_nxt = stateA, stateB
        P.wait("sync", "xis")
        rec_dv = {}
        for k in range(NCH):
            xsl = (k // 2) % 2
            pb = (k % 2) * 64
            if k % 2 == 0:
                P.gate("sync", f"xtm3_{xsl}")
                P.op("sync", lambda s, k=k, xsl=xsl: s.dma_start(
                    out=xtm3[xsl], in_=d_xin[:, k * L:(k + 2) * L], transpose=True),
                    inc=xts[xsl], n=16)
            P.wait("tensor", xts[xsl])
            P.wait("tensor", "dve")
            for dq in range(4):
                pd, pdn = ((psY0, "psY0"), (psY1, "psY1"))[dq % 2]
                P.gate("tensor", pdn)
                pv = P.op("tensor", lambda t, k=k, dq=dq, xsl=xsl, pd=pd, pb=pb: t.matmul(
                    pd[0:16, :], bd_sl(k),
                    xtm3[xsl][pb:pb + 64, dq * 512:(dq + 1) * 512], start=True, stop=True), inc="pe")
                if dq == 3 and k % 2 == 1:
                    P.set_gate(f"xtm3_{xsl}", "pe", pv)
                P.wait("vector", "pe", pv)
                dv = P.op("vector", lambda v, k=k, dq=dq, pd=pd, sc=st_cur, sn=st_nxt:
                          v.scalar_tensor_tensor(
                              sn[:, dq * 512:(dq + 1) * 512],
                              sc[:, dq * 512:(dq + 1) * 512],
                              pmat[:, k:k + 1], pd[0:16, :], ALU.mult, ALU.add), inc="dve")
                P.set_gate(pdn, "dve", dv)
            st_cur, st_nxt = st_nxt, st_cur
        P.wait("sync", "dve")
        P3_PE = P.cnt["pe"]
        P3_DVE = P.cnt["dve"]
        P3_ACT = P.cnt["acts"]
        sv = P.op("sync", lambda s, fs=st_cur: s.dma_start(out=d_state[:, :], in_=fs), inc="sts", n=16)



        if DEBUG and DEBUG_INLINE:
            P.wait("sync", "dve", RT_DONE)
            P.op("sync", lambda s: s.dma_start(out=o_rt[:, :], in_=dbg_rt.rearrange("p a b -> p (a b)")), inc="dbs", n=16)

        P.wait("gpsimd", "dve", RT_DONE)
        P.wait("gpsimd", "cc", CC_AG)
        P.wait("gpsimd", "dzr")
        REGH = {}

        def _mkreg(g):
            r = g.alloc_register()
            g.reg_mov(r, CAP - 1)
            REGH["b"] = r
        P.op("gpsimd", _mkreg)
        for gtb in range(32):
            ssl = gtb % 2
            sxs = ["sx0", "sx1"][ssl]
            P.gate("gpsimd", f"sxn{ssl}")
            P.op("gpsimd", lambda g, gtb=gtb, ssl=ssl: g.dma_start(
                out=sxn[ssl], in_=d_xn_all[gtb * 512:(gtb + 1) * 512, :].rearrange(
                    "(t p) d -> p t d", p=128)), inc=sxs, n=16)
            P.wait("gpsimd", sxs)
            for t4 in range(4):
                gt = gtb * 4 + t4
                for rk, sct in ((0, scidx0), (1, scidx1)):
                    pv = P.op("gpsimd", lambda g, gt=gt, t4=t4, ssl=ssl, sct=sct: g.indirect_dma_start(
                        out=d_disp, out_offset=bass.IndirectOffsetOnAxis(ap=sct[:, gt:gt + 1], axis=0),
                        in_=sxn[ssl][:, t4, :], in_offset=None,
                        bounds_check=REGH["b"], oob_is_err=False), inc="dsp", n=16)
            P.set_gate(f"sxn{ssl}", "dsp", pv)
        DISP_DONE = P.cnt["dsp"]
        P.wait("gpsimd", "sts")
        P.op("gpsimd", lambda g: g.collective_compute(
            "AllGather", ALU.bypass,
            replica_groups=[[0, 1], [2, 3], [4, 5], [6, 7]],
            ins=[d_state], outs=[d_state_ag]), inc="cc")
        CC_ST = P.cnt["cc"]
        P.wait("gpsimd", "pe", P3_PE)
        P.wait("gpsimd", "dve", P3_DVE)
        P.wait("gpsimd", "acts", P3_ACT)
        P.wait("gpsimd", "xt0")
        P.wait("gpsimd", "xt1")
        P.wait("gpsimd", "xis")
        for ch in range(8):
            P.op("gpsimd", lambda g, ch=ch: g.dma_start(
                out=wfc1[:, :, ch * 512:(ch + 1) * 512],
                in_=fc1t_in[:, ch * 512:(ch + 1) * 512].rearrange(
                    "(dc p) n -> p dc n", p=128)), inc="wf1", n=16)
        WFC1_LD = P.cnt["wf1"]
        P.wait("gpsimd", "dve", RT_DONE)
        for ch in range(8):
            P.op("gpsimd", lambda g, ch=ch: g.dma_start(
                out=wfc2[:, ch * 4:(ch + 1) * 4, :],
                in_=fc2_in[ch * 512:(ch + 1) * 512, :].rearrange(
                    "(hc p) d -> p hc d", p=128)), inc="wf2", n=16)
        WFC2_LD = P.cnt["wf2"]

        P.wait("sync", "dve", P3_DVE)
        P.wait("sync", "sts")
        P.wait("sync", "xis")
        sv = P.op("sync", lambda s: s.dma_start(out=fc2b_row, in_=fc2b_in[:, :]), inc="fcb", n=16)
        P.wait("tensor", "fcb")
        P.gate("tensor", "psO")
        P.op("tensor", lambda t: t.matmul(psO[:, 0:512], ones_rf, fc2b_row[:, 0:512], start=True, stop=True))
        pv = P.op("tensor", lambda t: t.matmul(psO[:, 512:1024], ones_rf, fc2b_row[:, 512:1024], start=True, stop=True), inc="pe")
        P.wait("vector", "pe", pv)
        P.wait("vector", "sts")
        P.wait("vector", "xis")
        P.wait("vector", "xt0")
        P.wait("vector", "xt1")
        dv = P.op("vector", lambda v: v.tensor_copy(fc2b_bc, psO[:, :]), inc="dve")
        P.set_gate("psO", "dve", dv)
        FC2B_BC = dv

        P.wait("tensor", "wf1", WFC1_LD)
        for blk in range(NBLK):
            xsl = blk % 2
            xfs = ["xf0", "xf1"][xsl]
            P.gate("sync", f"xfm{xsl}")
            if blk == 0:
                P.wait("sync", "dsp", DISP_DONE)
                P.wait("sync", "pe", P3_PE)
                P.wait("sync", "dve", P3_DVE)
                P.wait("sync", "acts", P3_ACT)
                P.wait("sync", "xt0")
                P.wait("sync", "xt1")
                P.wait("scalar", "dve", P3_DVE)
                P.wait("scalar", "pe", P3_PE)
                P.wait("scalar", "sts")
                P.wait("scalar", "xis")
            for dc in range(8):
                P.op("sync", lambda s, blk=blk, dc=dc, xsl=xsl: s.dma_start(
                    out=xfm[xsl][:, dc, :],
                    in_=d_disp[blk * MBLK:(blk + 1) * MBLK, dc * 128:(dc + 1) * 128],
                    transpose=True), inc=xfs, n=16)
            P.wait("tensor", xfs)
            pe_last = None
            for ht in range(32):
                pp, ppn = ((psA, "psA"), (psB, "psB"))[ht % 2]
                P.gate("tensor", ppn)
                if ht == 0:
                    P.gate("tensor", "h_fm")
                for dc in range(8):
                    pv = P.op("tensor", lambda t, ht=ht, dc=dc, pp=pp, xsl=xsl: t.matmul(
                        pp[:, 0:MBLK], wfc1[:, dc, ht * 128:(ht + 1) * 128],
                        xfm[xsl][:, dc, :], start=(dc == 0), stop=(dc == 7)),
                        inc="pe" if dc == 7 else None)
                pe_last = pv
                P.wait("scalar", "pe", pv)
                av = P.op("scalar", lambda s, ht=ht, pp=pp: s.activation(
                    h_fm[:, ht, :], pp[:, 0:MBLK], AF.Silu,
                    bias=fc1b_sb[:, ht:ht + 1]), inc="acts")
                P.set_gate(ppn, "acts", av)
            P.set_gate(f"xfm{xsl}", "pe", pe_last)
            P.wait("tensor", "acts")
            if blk == 0:
                P.wait("tensor", "wf2", WFC2_LD)
            for st2 in range(2):
                tt = blk * 2 + st2
                zsl = st2 % 2
                for half in range(2):
                    pq, pqn = ((psY0, "psY0"), (psY1, "psY1"))[half]
                    P.gate("tensor", pqn)
                    for hc in range(32):
                        pv = P.op("tensor", lambda t, st2=st2, half=half, hc=hc, pq=pq:
                                  t.matmul(pq, h_fm[:, hc, st2 * 128:(st2 + 1) * 128],
                                           wfc2[:, hc, half * 512:(half + 1) * 512],
                                           start=(hc == 0), stop=(hc == 31)),
                                  inc="pe" if hc == 31 else None)
                    P.wait("vector", "pe", pv)
                    if blk == 0 and st2 == 0 and half == 0:
                        P.wait("vector", "dve", FC2B_BC)
                    P.gate("vector", f"z_sb{zsl}")
                    dv = P.op("vector", lambda v, half=half, zsl=zsl, pq=pq: v.tensor_tensor(
                        y_sb[zsl][:, half * 512:(half + 1) * 512], pq,
                        fc2b_bc[:, half * 512:(half + 1) * 512], ALU.add), inc="dve")
                    P.set_gate(pqn, "dve", dv)
                P.wait("sync", "dve", dv)
                sv = P.op("sync", lambda s, tt=tt, zsl=zsl: s.dma_start(
                    out=d_y[tt * 128:(tt + 1) * 128, :], in_=y_sb[zsl]), inc="yst", n=16)
                P.set_gate(f"z_sb{zsl}", "yst", sv)
            P.set_gate("h_fm", "pe", P.cnt["pe"])
        Y_ST = P.cnt["yst"]
        MO_PE = P.cnt["pe"]

        P.wait("gpsimd", "yst")
        P.op("gpsimd", lambda g: g.collective_compute(
            "AllGather", ALU.bypass, replica_groups=[list(range(8))],
            ins=[d_y], outs=[d_y_all]), inc="cc")
        CC_Y = P.cnt["cc"]

        P.wait("gpsimd", "pe", MO_PE)
        P.wait("gpsimd", "yst")
        P.wait("scalar", "yst")
        for ch in range(4):
            P.op("gpsimd", lambda g, ch=ch: g.dma_start(
                out=woutsb[:, ch * 4:(ch + 1) * 4, :],
                in_=w_out[ch * 512:(ch + 1) * 512, :].rearrange(
                    "(dc p) d -> p dc d", p=128)), inc="wo", n=16)
        WOUT_LD = P.cnt["wo"]
        P.wait("sync", "cc", CC_ST)
        P.wait("sync", "yst")
        iv = P.op("sync", lambda s: s.dma_start(out=init_st, in_=d_state_ag[0:NS, :]), inc="ini", n=16)
        P.wait("vector", "ini")
        P.wait("vector", "yst")
        P.wait("vector", "pe", MO_PE)
        P.op("vector", lambda v: v.tensor_scalar(
            init_st, init_st, isoddv[0:16, 0:1], None, ALU.mult), inc="dve")
        P.wait("vector", "dve")
        P.op("vector", lambda v: v.tensor_copy(st6A, init_st), inc="dve")
        st_cur, st_nxt = st6A, st6B
        P.wait("tensor", "wo", WOUT_LD)
        rec_last = P.cnt["dve"]
        ev_gate = {0: None, 1: None}
        for k in range(NCH):
            xsl = (k // 2) % 2
            pb = (k % 2) * 64
            par = (k % 2) if P6PP else 0

            def po(j, k=k):
                if (k % 2 == 0) or not P6PP:
                    return psO[:, j * 64:(j + 1) * 64]
                return (psA if j < 8 else psB)[:, (j % 8) * 64:(j % 8 + 1) * 64]
            if k % 2 == 0:
                P.gate("sync", f"xtm6_{xsl}")
                P.op("sync", lambda s, k=k, xsl=xsl: s.dma_start(
                    out=xtm6[xsl], in_=d_xin[:, k * L:(k + 2) * L], transpose=True),
                    inc=xts[2 + xsl], n=16)
            P.wait("tensor", xts[2 + xsl])
            P.wait("tensor", "dve", rec_last)
            if par == 0:
                P.gate("tensor", "psO")
            else:
                P.gate("tensor", "psA")
                P.gate("tensor", "psB")
            for j in range(16):
                P.op("tensor", lambda t, k=k, j=j, xsl=xsl, pb=pb, po=po,
                     st=(j in (0, 8)): t.matmul(
                    po(j), xtm6[xsl][pb:pb + 64, j * 128:(j + 1) * 128],
                    m2_sl(k), start=st, stop=False))
            for j in range(16):
                pv = P.op("tensor", lambda t, k=k, j=j, sc=st_cur, po=po,
                          sp=(j in (7, 15)): t.matmul(
                    po(j), sc[:, j * 128:(j + 1) * 128],
                    ctp_all[:, k * 64:(k + 1) * 64], start=False, stop=sp),
                    inc="pe" if j == 15 else None)
            out_pe = pv
            for dq in range(4):
                pd, pdn = ((psY0, "psY0"), (psY1, "psY1"))[dq % 2]
                P.gate("tensor", pdn)
                pv = P.op("tensor", lambda t, k=k, dq=dq, xsl=xsl, pd=pd, pb=pb: t.matmul(
                    pd[0:16, :], bd_sl(k),
                    xtm6[xsl][pb:pb + 64, dq * 512:(dq + 1) * 512], start=True, stop=True), inc="pe")
                if dq == 3 and k % 2 == 1:
                    P.set_gate(f"xtm6_{xsl}", "pe", pv)
                P.wait("vector", "pe", pv)
                dv = P.op("vector", lambda v, k=k, dq=dq, pd=pd, sc=st_cur, sn=st_nxt:
                          v.scalar_tensor_tensor(
                              sn[:, dq * 512:(dq + 1) * 512],
                              sc[:, dq * 512:(dq + 1) * 512],
                              pmat[:, k:k + 1], pd[0:16, :], ALU.mult, ALU.add), inc="dve")
                P.set_gate(pdn, "dve", dv)
            rec_last = dv
            st_cur, st_nxt = st_nxt, st_cur
            P.wait("scalar", "pe", out_pe)
            P.gate("scalar", "sq_sb")
            for j in range(16):
                av = P.op("scalar", lambda s, j=j, k=k, po=po: s.activation(
                    sq_sb[:, j * 64:(j + 1) * 64], po(j),
                    AF.Square), inc="acts" if j == 15 else None)
            P.wait("tensor", "acts", av)
            P.gate("tensor", "psM")
            for j in range(16):
                pv = P.op("tensor", lambda t, j=j: t.matmul(
                    psM[0:1, 0:64], onescf,
                    sq_sb[:, j * 64:(j + 1) * 64], start=(j == 0), stop=(j == 15)),
                    inc="pe" if j == 15 else None)
            P.set_gate("sq_sb", "pe", pv)
            P.wait("scalar", "pe", pv)
            P.gate("scalar", "rms_row")
            P.op("scalar", lambda s: s.activation(
                rms_row, psM[0:1, 0:64], AF.Ln, scale=1.0 / DI, bias=epsrms[0:1, 0:1]), inc="acts")
            P.set_gate("psM", "acts", P.cnt["acts"])
            P.wait("scalar", "acts")
            P.op("scalar", lambda s: s.activation(
                rms_row, rms_row, AF.Exp, scale=-0.5), inc="acts")
            P.wait("tensor", "acts")
            P.gate("tensor", "psT")
            pv = P.op("tensor", lambda t: t.matmul(
                psT[:, 128:192], ones_rf, rms_row, start=True, stop=True), inc="pe")
            P.set_gate("rms_row", "pe", pv)
            P.wait("vector", "pe", pv)
            P.gate("vector", "rstd_bc")
            P.op("vector", lambda v: v.tensor_copy(rstd_bc, psT[:, 128:192]), inc="dve")
            P.set_gate("psT", "dve", P.cnt["dve"])
            P.wait("vector", "dve")
            P.gate("vector", "out_nrm")
            colk = (k % 2) * 64
            for j in range(16):
                dv = P.op("vector", lambda v, j=j, colk=colk, k=k, po=po: v.scalar_tensor_tensor(
                    out_nrm[:, j, colk:colk + 64], po(j),
                    normw[:, j:j + 1], rstd_bc, ALU.mult, ALU.mult),
                    inc="dve" if j == 15 else None)
            P.set_gate("psO" if par == 0 else "psA", "dve", dv)
            P.set_gate("rstd_bc", "dve", dv)
            if k % 2 == 1:
                P.wait("tensor", "dve", dv)
                for half in range(2):
                    pq, pqn = ((psY0, "psY0"), (psY1, "psY1"))[half]
                    P.gate("tensor", pqn)
                    for dc in range(16):
                        pv = P.op("tensor", lambda t, dc=dc, half=half, pq=pq: t.matmul(
                            pq, out_nrm[:, dc, :],
                            woutsb[:, dc, half * 512:(half + 1) * 512],
                            start=(dc == 0), stop=(dc == 15)),
                            inc="pe" if dc == 15 else None)
                    P.wait("vector", "pe", pv)
                    dv2 = P.op("vector", lambda v, k=k, half=half, pq=pq: v.tensor_copy(
                        ssm_tm[:, k // 2, half * 512:(half + 1) * 512], pq), inc="dve")
                    P.set_gate(pqn, "dve", dv2)
                P.set_gate("out_nrm", "pe", pv)
        SSM_DV = P.cnt["dve"]
        P6_PE = P.cnt["pe"]

        P.wait("sync", "cc", CC_Y)
        P.wait("sync", "pe", P6_PE)
        P.wait("scalar", "pe", P6_PE)
        P.wait("gpsimd", "cc", CC_Y)
        P.wait("gpsimd", "pe", P6_PE)
        if DEBUG and DEBUG_INLINE:
            P.op("sync", lambda s: s.dma_start(out=o_m2[:, :], in_=m2m_all), inc="dbs", n=16)
            P.op("sync", lambda s: s.dma_start(out=o_ctp[:, :], in_=ctp_all), inc="dbs", n=16)
            P.op("sync", lambda s: s.dma_start(out=o_bd[:, :], in_=bd_all), inc="dbs", n=16)
            P.op("sync", lambda s: s.dma_start(out=o_pm[:, :], in_=pmat), inc="dbs", n=16)
        sv = P.op("sync", lambda s: s.dma_start(out=gam_row, in_=gamma_in[:, :]), inc="gbl", n=16)
        sv = P.op("sync", lambda s: s.dma_start(out=bet_row, in_=beta_in[:, :]), inc="gbl", n=16)
        P.wait("tensor", "gbl")
        P.wait("tensor", "pe", P6_PE)
        for (row, dst) in ((gam_row, gb_bc[:, 0:1024]), (bet_row, gb_bc[:, 1024:2048])):
            P.gate("tensor", "psO")
            P.op("tensor", lambda t, row=row: t.matmul(psO[:, 0:512], ones_rf, row[:, 0:512], start=True, stop=True))
            pv = P.op("tensor", lambda t, row=row: t.matmul(psO[:, 512:1024], ones_rf, row[:, 512:1024], start=True, stop=True), inc="pe")
            P.wait("vector", "pe", pv)
            dv = P.op("vector", lambda v, dst=dst: v.tensor_copy(dst, psO[:, :]), inc="dve")
            P.set_gate("psO", "dve", dv)
        GB_BC = P.cnt["dve"]
        fin_dv = {}
        cgs = ["cg0", "cg1", "cg2", "cg3"]
        for i in range(NT):
            sl = lxs[i % 2]
            if i >= 2:
                P.wait("sync", "dve", fin_dv[i - 2])
            P.op("sync", lambda s, i=i: s.dma_start(
                out=xring[i % 2], in_=x_in[i * 128:(i + 1) * 128, :]), inc=sl, n=16)
            s0, s1 = (2 * i) % 4, (2 * i + 1) % 4
            P.gate("gpsimd", f"ybuf{s0}")
            P.op("gpsimd", lambda g, i=i, s0=s0: g.indirect_dma_start(
                out=ybuf[s0][:, :], out_offset=None, in_=d_y_all,
                in_offset=bass.IndirectOffsetOnAxis(ap=cidx_loc[:, i:i + 1], axis=0)),
                inc=cgs[s0], n=16)
            P.gate("gpsimd", f"ybuf{s1}")
            P.op("gpsimd", lambda g, i=i, s1=s1: g.indirect_dma_start(
                out=ybuf[s1][:, :], out_offset=None, in_=d_y_all,
                in_offset=bass.IndirectOffsetOnAxis(ap=cidx_loc[:, 16 + i:17 + i], axis=0)),
                inc=cgs[s1], n=16)
            P.wait("scalar", sl)
            P.gate("scalar", "lnv0")
            P.op("scalar", lambda s, b=xring[i % 2]: s.activation(
                sqdump, b, AF.Square, accum_out=lnv[0]), inc="acts")
            ac_sq = P.cnt["acts"]
            P.wait("vector", sl)
            if i == 0:
                P.wait("vector", "dve", max(SSM_DV, GB_BC))
            P.op("vector", lambda v, b=xring[i % 2]: v.tensor_reduce(
                lnv[1], b, AX.X, ALU.add), inc="dve")
            P.wait("vector", "dve")
            P.op("vector", lambda v: v.tensor_scalar(lnv[2], lnv[1], 1.0 / D, None, ALU.mult), inc="dve")
            P.wait("vector", "dve")
            P.op("vector", lambda v: v.tensor_tensor(lnv[3], lnv[2], lnv[2], ALU.mult), inc="dve")
            P.wait("vector", "acts", ac_sq)
            P.wait("vector", "dve")
            dv = P.op("vector", lambda v: v.scalar_tensor_tensor(
                lnv[4], lnv[0], 1.0 / D, lnv[3], ALU.mult, ALU.subtract), inc="dve")
            P.set_gate("lnv0", "dve", dv)
            P.wait("scalar", "dve", dv)
            P.op("scalar", lambda s: s.activation(lnv[5], lnv[4], AF.Sqrt, bias=epsln[:, 0:1]), inc="acts")
            P.wait("vector", "acts")
            P.op("vector", lambda v: v.reciprocal(lnv[4], lnv[5]), inc="dve")
            P.wait("vector", "dve")
            P.op("vector", lambda v, b=xring[i % 2]: v.tensor_scalar(
                xnf, b, lnv[2][:, 0:1], lnv[4][:, 0:1], ALU.subtract, ALU.mult), inc="dve")
            P.wait("vector", cgs[s0])
            P.wait("vector", cgs[s1])
            P.wait("vector", "dve")
            P.op("vector", lambda v, i=i, s0=s0: v.tensor_scalar(
                moe_t, ybuf[s0], g_loc[:, i:i + 1], None, ALU.mult), inc="dve")
            P.wait("vector", "dve")
            dv = P.op("vector", lambda v, i=i, s1=s1: v.scalar_tensor_tensor(
                moe_t, ybuf[s1], g_loc[:, 16 + i:17 + i], moe_t, ALU.mult, ALU.add), inc="dve")
            P.set_gate(f"ybuf{s0}", "dve", dv)
            P.set_gate(f"ybuf{s1}", "dve", dv)
            P.wait("vector", "dve")
            P.op("vector", lambda v: v.tensor_tensor(res_t, xnf, moe_t, ALU.add), inc="dve")
            P.wait("vector", "dve")
            P.wait("scalar", "dve")
            P.gate("scalar", "lnv0")
            P.op("scalar", lambda s: s.activation(sqdump, res_t, AF.Square, accum_out=lnv[0]), inc="acts")
            ac_sq2 = P.cnt["acts"]
            P.op("vector", lambda v: v.tensor_reduce(lnv[1], res_t, AX.X, ALU.add), inc="dve")
            P.wait("vector", "dve")
            P.op("vector", lambda v: v.tensor_scalar(lnv[2], lnv[1], 1.0 / D, None, ALU.mult), inc="dve")
            P.wait("vector", "dve")
            P.op("vector", lambda v: v.tensor_tensor(lnv[3], lnv[2], lnv[2], ALU.mult), inc="dve")
            P.wait("vector", "acts", ac_sq2)
            P.wait("vector", "dve")
            dv = P.op("vector", lambda v: v.scalar_tensor_tensor(
                lnv[4], lnv[0], 1.0 / D, lnv[3], ALU.mult, ALU.subtract), inc="dve")
            P.set_gate("lnv0", "dve", dv)
            P.wait("scalar", "dve", dv)
            P.op("scalar", lambda s: s.activation(lnv[5], lnv[4], AF.Sqrt, bias=epsln[:, 0:1]), inc="acts")
            P.wait("vector", "acts")
            P.op("vector", lambda v: v.reciprocal(lnv[4], lnv[5]), inc="dve")
            P.wait("vector", "dve")
            P.op("vector", lambda v: v.tensor_scalar(
                ffn_t, res_t, lnv[2][:, 0:1], lnv[4][:, 0:1], ALU.subtract, ALU.mult), inc="dve")
            P.wait("vector", "dve")
            P.op("vector", lambda v: v.tensor_tensor(ffn_t, ffn_t, gb_bc[:, 0:1024], ALU.mult), inc="dve")
            P.wait("vector", "dve")
            P.op("vector", lambda v: v.tensor_tensor(ffn_t, ffn_t, gb_bc[:, 1024:2048], ALU.add), inc="dve")
            P.wait("vector", "dve")
            P.wait("vector", "outs")
            P.op("vector", lambda v, i=i, b=xring[i % 2]: v.tensor_tensor(
                fin_t, b, ssm_tm[:, i, :], ALU.add), inc="dve")
            P.wait("vector", "dve")
            dv = P.op("vector", lambda v: v.tensor_tensor(fin_t, fin_t, ffn_t, ALU.add), inc="dve")
            fin_dv[i] = dv
            P.wait("sync", "dve", dv)
            P.op("sync", lambda s, i=i: s.dma_start(
                out=o_out[i * 128:(i + 1) * 128, :], in_=fin_t), inc="outs", n=16)
        if DEBUG:
            P.wait("gpsimd", "outs")
            P.op("gpsimd", lambda g: g.dma_start(out=o_xnb[:, :], in_=d_xnb[:, :]), inc="dbg", n=16)
            P.op("gpsimd", lambda g: g.dma_start(out=o_scl[:, :], in_=d_sc_all[:, :]), inc="dbg", n=16)
            P.op("gpsimd", lambda g: g.dma_start(out=o_xin[:, :], in_=d_xin[:, :]), inc="dbg", n=16)
            P.op("gpsimd", lambda g: g.dma_start(out=o_st[:, :], in_=d_state_ag[:, :]), inc="dbg", n=16)
            P.op("gpsimd", lambda g: g.dma_start(out=o_moe[:, :], in_=d_y[:, :]), inc="dbg", n=16)
            P.op("gpsimd", lambda g: g.dma_start(
                out=o_ssm.rearrange("(i p) d -> p i d", p=128),
                in_=ssm_tm), inc="dbg", n=16)
            P.wait("gpsimd", "dbg")
            P.wait("sync", "dbs")
        P.wait("sync", "outs")
        P.wait("gpsimd", "outs")

        for name in P.cnt:
            P.sems[name] = EC(nc.semaphore(name))
        block = EC(nc.Block())

        @block.sync
        def _(e):
            for f in P.q["sync"]:
                f(e)

        @block.gpsimd
        def _(e):
            for f in P.q["gpsimd"]:
                f(e)

        @block.vector
        def _(e):
            for f in P.q["vector"]:
                f(e)

        @block.scalar
        def _(e):
            for f in P.q["scalar"]:
                f(e)

        @block.tensor
        def _(e):
            for f in P.q["tensor"]:
                f(e)

    return nc


_NC_CACHE = None
_LAST_RES = None
_LAST_PLAN = None


def kernel(L0_attn_x, L0_attn_W_in, L0_attn_W_A, L0_attn_W_B, L0_attn_W_C,
           L0_attn_W_out, L0_attn_norm_w, L0_ffn_wg, L0_ffn_fc1_w, L0_ffn_fc1_b,
           L0_ffn_fc2_w, L0_ffn_fc2_b, L0_ffn_ln_gamma, L0_ffn_ln_beta):
    global _NC_CACHE
    x = np.ascontiguousarray(np.asarray(L0_attn_x, dtype=np.float32).reshape(NTOK, D))
    w_in_v = np.ascontiguousarray(np.asarray(L0_attn_W_in, dtype=np.float32))
    wa = np.asarray(L0_attn_W_A, dtype=np.float32)
    wb = np.asarray(L0_attn_W_B, dtype=np.float32)
    wc = np.asarray(L0_attn_W_C, dtype=np.float32)
    wabc_v = np.zeros((DI, 80), np.float32)
    wabc_v[:, 0:16] = wa
    wabc_v[:, 32:48] = wb
    wabc_v[:, 64:80] = wc
    w_out_v = np.ascontiguousarray(np.asarray(L0_attn_W_out, dtype=np.float32))
    norm_w_v = np.ascontiguousarray(np.asarray(L0_attn_norm_w, dtype=np.float32))
    wg_v = np.ascontiguousarray(np.asarray(L0_ffn_wg, dtype=np.float32))
    fc1 = np.asarray(L0_ffn_fc1_w, dtype=np.float32)
    fc1t = np.ascontiguousarray(fc1.transpose(0, 2, 1))
    fc1b = np.ascontiguousarray(np.asarray(L0_ffn_fc1_b, dtype=np.float32))
    fc2 = np.ascontiguousarray(np.asarray(L0_ffn_fc2_w, dtype=np.float32))
    fc2b = np.ascontiguousarray(np.asarray(L0_ffn_fc2_b, dtype=np.float32))
    gam = np.ascontiguousarray(np.asarray(L0_ffn_ln_gamma, dtype=np.float32).reshape(1, D))
    bet = np.ascontiguousarray(np.asarray(L0_ffn_ln_beta, dtype=np.float32).reshape(1, D))

    if _NC_CACHE is None:
        _NC_CACHE = build()
    nc = _NC_CACHE

    in_maps = []
    for c in range(8):
        consts = np.zeros((1, 8), np.float32)
        consts[0, 0] = float(c % 2)
        consts[0, 1] = float(c)
        in_maps.append({
            "x": np.ascontiguousarray(x[c * TOK:(c + 1) * TOK]),
            "w_in": w_in_v,
            "wabc": wabc_v,
            "w_out": w_out_v,
            "wg": wg_v,
            "norm_w": np.ascontiguousarray(norm_w_v.reshape(16, 128).T),
            "fc1t": np.ascontiguousarray(fc1t[c]),
            "fc1b": np.ascontiguousarray(fc1b[c].reshape(32, 128).T),
            "fc2": np.ascontiguousarray(fc2[c]),
            "fc2b": np.ascontiguousarray(fc2b[c].reshape(1, D)),
            "gamma": gam,
            "beta": bet,
            "consts": consts,
        })
    res = run_bass_kernel_spmd(nc, in_maps, core_ids=list(range(8)))
    global _LAST_RES
    _LAST_RES = res
    out = np.concatenate([np.asarray(r["out"]) for r in res.results], axis=0)
    return out.reshape(B, T, D)


# version 13
# speedup vs baseline: 1.1065x; 1.1065x over previous
import numpy as np
import concourse.bass as bass
import concourse.mybir as mybir
from concourse.bass_utils import run_bass_kernel_spmd
from contextlib import ExitStack

F32 = mybir.dt.float32
BF16 = mybir.dt.bfloat16
U8 = mybir.dt.uint8
U32 = mybir.dt.uint32
AF = mybir.ActivationFunctionType
ALU = mybir.AluOpType
AX = mybir.AxisListType

B, T, D = 4, 4096, 1024
DI = 2048
NS = 16
E, H = 8, 4096
NTOK = B * T
TOK = NTOK // 8
NT = TOK // 128
L = 64
NCH = TOK // L
CAP = 5120
EPS_TOP = 1.1920929e-07
LN_EPS = 1e-5
RMS_EPS = float(np.finfo(np.float32).eps)
MBLK = 256
P6PP = True
NBLK = CAP // MBLK

DT_SZ = {F32: 4, BF16: 2, U8: 1, U32: 4}


class Plan:
    ENGINES = ("sync", "gpsimd", "vector", "scalar", "tensor")

    def __init__(self):
        self.q = {e: [] for e in self.ENGINES}
        self.sems = {}
        self.cnt = {}
        self.gates = {}

    def sem(self, *names):
        for n in names:
            self.cnt[n] = 0

    def wait(self, eng, sem, val=None):
        v = self.cnt[sem] if val is None else val
        if v <= 0:
            return
        self.q[eng].append(lambda e, s=sem, vv=v: e.wait_ge(self.sems[s], vv))

    def op(self, eng, fn, inc=None, n=1):
        if inc is not None:
            self.cnt[inc] += n
            self.q[eng].append(
                lambda e, f=fn, s=inc, k=n: f(e).then_inc(self.sems[s], k))
            return self.cnt[inc]
        self.q[eng].append(fn)
        return None

    DMA_SEMS = ("sld", "sst", "gld", "xns", "scs", "xis", "sts", "zst",
                "wf1", "wf2", "wo", "ini", "fcb", "gbl", "dbs", "outs",
                "dsp", "yst", "sx0", "sx1", "cg0", "cg1", "cg2", "cg3", "cg4", "cg5", "cg6", "cg7", "dzr", "ab0", "ab1",
                "lq0", "lq1", "lq2", "lq3",
                "lx0", "lx1", "mz0", "mz1", "xf0", "xf1",
                "wi0", "wi1", "wi2", "wi3")

    def gate(self, eng, res):
        g = self.gates.get(res)
        if g is not None:
            if g[0] in self.DMA_SEMS:
                self.wait(eng, g[0])
            else:
                self.wait(eng, g[0], g[1])

    def set_gate(self, res, sem, val):
        self.gates[res] = (sem, val)


class SBAlloc:

    def __init__(self, arena_ap, total):
        self.a = arena_ap
        self.total = total
        self.items = []

    def alloc(self, name, shape, dtype, phases):
        n = 1
        for s in shape[1:]:
            n *= s
        nbytes = n * DT_SZ[dtype]
        ph = frozenset(phases)
        off = 0
        while True:
            off = (off + 63) // 64 * 64
            conflict = None
            for (s0, e0, p0, _nm) in self.items:
                if s0 < off + nbytes and off < e0 and (ph & p0 or "C" in ph or "C" in p0):
                    conflict = e0
            if conflict is None:
                break
            off = conflict
        assert off + nbytes <= self.total, (name, off, nbytes)
        self.items.append((off, off + nbytes, ph, name))
        v = self.a[:, off:off + nbytes].bitcast(dtype)
        if len(shape) == 2:
            if shape[0] != 128:
                v = v[0:shape[0], :]
        elif len(shape) == 3:
            v = v.rearrange("p (a b) -> p a b", a=shape[1])
            if shape[0] != 128:
                v = v[0:shape[0], :, :]
        else:
            raise ValueError(shape)
        return v


DEBUG = False
DEBUG_INLINE = True


def build():
    nc = bass.Bass("TRN2", target_bir_lowering=False)

    x_in = nc.dram_tensor("x", [TOK, D], F32, kind="ExternalInput").ap()
    w_in = nc.dram_tensor("w_in", [D, 2 * DI], F32, kind="ExternalInput").ap()
    wabc = nc.dram_tensor("wabc", [DI, 80], F32, kind="ExternalInput").ap()
    w_out = nc.dram_tensor("w_out", [DI, D], F32, kind="ExternalInput").ap()
    wg_in = nc.dram_tensor("wg", [D, E], F32, kind="ExternalInput").ap()
    normw_in = nc.dram_tensor("norm_w", [128, 16], F32, kind="ExternalInput").ap()
    fc1t_in = nc.dram_tensor("fc1t", [D, H], F32, kind="ExternalInput").ap()
    fc1b_in = nc.dram_tensor("fc1b", [128, 32], F32, kind="ExternalInput").ap()
    fc2_in = nc.dram_tensor("fc2", [H, D], F32, kind="ExternalInput").ap()
    fc2b_in = nc.dram_tensor("fc2b", [1, D], F32, kind="ExternalInput").ap()
    gamma_in = nc.dram_tensor("gamma", [1, D], F32, kind="ExternalInput").ap()
    beta_in = nc.dram_tensor("beta", [1, D], F32, kind="ExternalInput").ap()
    consts_in = nc.dram_tensor("consts", [1, 8], F32, kind="ExternalInput").ap()
    o_out = nc.dram_tensor("out", [TOK, D], F32, kind="ExternalOutput").ap()
    if DEBUG:
        o_xnb = nc.dram_tensor("o_xnb", [TOK, D], BF16, kind="ExternalOutput").ap()
        o_scl = nc.dram_tensor("o_scl", [NTOK, E], F32, kind="ExternalOutput").ap()
        o_xin = nc.dram_tensor("o_xin", [DI, TOK], BF16, kind="ExternalOutput").ap()
        o_st = nc.dram_tensor("o_st", [2 * NS, DI], F32, kind="ExternalOutput").ap()
        o_moe = nc.dram_tensor("o_moe", [CAP, D], BF16, kind="ExternalOutput").ap()
        o_ssm = nc.dram_tensor("o_ssm", [TOK, D], BF16, kind="ExternalOutput").ap()
        o_rt = nc.dram_tensor("o_rt", [128, 16 * 128], F32, kind="ExternalOutput").ap()
        o_m2 = nc.dram_tensor("o_m2", [128, (NCH // 2) * 64], BF16, kind="ExternalOutput").ap()
        o_ctp = nc.dram_tensor("o_ctp", [16, NCH * 64], F32, kind="ExternalOutput").ap()
        o_bd = nc.dram_tensor("o_bd", [128, (NCH // 2) * 16], BF16, kind="ExternalOutput").ap()
        o_pm = nc.dram_tensor("o_pm", [16, NCH], F32, kind="ExternalOutput").ap()
        o_se = nc.dram_tensor("o_se", [16, TOK], F32, kind="ExternalOutput").ap()
        o_so = nc.dram_tensor("o_so", [128, NCH * 1024], BF16, kind="ExternalOutput").ap()

    d_xnb = nc.dram_tensor("d_xnb", [TOK, D], BF16).ap()
    d_xnlo = nc.dram_tensor("d_xnlo", [TOK, D], BF16).ap()
    d_xn_all = nc.dram_tensor("d_xn_all", [NTOK, D], BF16).ap()
    d_scl = nc.dram_tensor("d_scl", [TOK, E], F32).ap()
    d_sc_all = nc.dram_tensor("d_sc_all", [NTOK, E], F32).ap()
    d_xin = nc.dram_tensor("d_xin", [DI, TOK], BF16).ap()
    d_state = nc.dram_tensor("d_state", [NS, DI], F32).ap()
    d_state_ag = nc.dram_tensor("d_state_ag", [2 * NS, DI], F32).ap()
    d_disp = nc.dram_tensor("d_disp", [CAP, D], BF16).ap()
    d_y = nc.dram_tensor("d_y", [CAP, D], BF16).ap()
    d_y_all = nc.dram_tensor("d_y_all", [E * CAP, D], BF16).ap()

    P = Plan()
    global _LAST_PLAN
    _LAST_PLAN = P
    P.sem("sld", "sst", "gld", "cc", "pe", "dve", "acts", "gpc", "outs")
    P.sem("xns", "scs", "xis", "sts", "zst", "wf1", "wf2", "wo", "ini", "fcb", "gbl")
    P.sem("dsp", "yst", "ys2", "sx0", "sx1", "cg0", "cg1", "cg2", "cg3", "cg4", "cg5", "cg6", "cg7", "dzr", "ab0", "ab1")
    P.sem("lx0", "lx1", "mz0", "mz1", "xf0", "xf1")
    P.sem("lq0", "lq1", "lq2", "lq3")
    lxs4 = ["lq0", "lq1", "lq2", "lq3"]
    P.sem("wi0", "wi1", "wi2", "wi3")
    P.sem("xt0", "xt1", "xt2", "xt3")
    xts = ["xt0", "xt1", "xt2", "xt3"]
    if DEBUG:
        P.sem("dbg")
        P.sem("dbs")
    wis = ["wi0", "wi1", "wi2", "wi3"]
    lxs = ["lx0", "lx1"]

    with ExitStack() as es:
        EC = es.enter_context
        SBT = 204 * 1024
        arena_t = EC(nc.sbuf_tensor("arena", [128, SBT], U8))
        A = SBAlloc(arena_t.ap(), SBT)
        ALLPH = ["C"]

        def phs(*p):
            return ALLPH if DEBUG else list(p)

        icol = A.alloc("icol", [128, 1], F32, ["C"])
        iota8 = A.alloc("iota8", [128, 8], F32, ["C"])
        onesb = A.alloc("onesb", [128, 1], BF16, ["C"])
        onescf = A.alloc("onescf", [128, 1], F32, ["C"])
        epsln = A.alloc("epsln", [128, 1], F32, ["C"])
        epsrms = A.alloc("epsrms", [1, 1], F32, ["C"])
        ones_rf = A.alloc("ones_rf", [1, 128], F32, ["C"])
        constr = A.alloc("constr", [1, 8], F32, ["C"])
        myev = A.alloc("myev", [128, 1], F32, ["C"])
        isoddv = A.alloc("isoddv", [128, 1], F32, ["C"])
        wg_hi = A.alloc("wg_hi", [128, 8, 8], BF16, ["C"])
        wg_lo = A.alloc("wg_lo", [128, 8, 8], BF16, ["C"])
        normw = A.alloc("normw", [128, 16], F32, ["C"])
        fc1b_sb = A.alloc("fc1b_sb", [128, 32], F32, ["C"])
        fc2b_row = A.alloc("fc2b_row", [1, 1024], F32, ["MO"])
        gam_row = A.alloc("gam_row", [1, 1024], F32, ["P7"])
        bet_row = A.alloc("bet_row", [1, 1024], F32, ["P7"])
        identb = A.alloc("identb", [128, 128], BF16, ["C"])
        identf = A.alloc("identf", [128, 128], F32, ["C"])
        tril_b = A.alloc("tril_b", [128, 128], BF16, ["C"])
        irow128 = A.alloc("irow128", [128, 128], F32, ["C"])
        zeros128 = A.alloc("zeros128", [128, 128], F32, ["C"])
        wabc_sb = A.alloc("wabc_sb", [128, 16, 80], BF16, ["C"])
        m2m_all = A.alloc("m2m_all", [128, (NCH // 2) * 64], BF16, ["P0", "SC", "P3", "MO", "P6"])
        ctp_all = A.alloc("ctp_all", [16, NCH * 64], F32, ["P0", "SC", "P3", "MO", "P6"])
        bd_all = A.alloc("bd_all", [128, (NCH // 2) * 16], BF16, ["P0", "SC", "P3", "MO", "P6"])
        pmat = A.alloc("pmat", [16, NCH], F32, ["P0", "SC", "P3", "MO", "P6"])
        stateA = A.alloc("stateA", [16, DI], F32, ["P3"])
        stateB = A.alloc("stateB", [16, DI], F32, ["P3"])
        st6A = A.alloc("st6A", [16, DI], F32, ["P6"])
        st6B = A.alloc("st6B", [16, DI], F32, ["P6"])
        zrow = A.alloc("zrow", [128, 2048], U8, ["C"])
        scidx0 = A.alloc("scidx0", [128, 128], U32, ["P3", "MO"])
        scidx1 = A.alloc("scidx1", [128, 128], U32, ["P3", "MO"])
        ohmy = A.alloc("ohmy", [128, 16], F32, ["P3"])
        mycol = A.alloc("mycol", [128, 1], F32, ["P3"])
        cidx_loc = A.alloc("cidx_loc", [128, 32], U32, ["P3", "MO", "P6", "P7"])
        g_loc = A.alloc("g_loc", [128, 32], F32, ["P3", "MO", "P6", "P7"])
        sxn = [A.alloc(f"sxn{i}", [128, 4, 1024], BF16, ["P3", "MO"]) for i in range(2)]
        ybuf = [A.alloc(f"ybuf{i}", [128, 1024], BF16, ["P7"]) for i in range(8)]
        fc2b_bc = A.alloc("fc2b_bc", [128, 1024], F32, ["MO"])
        gb_bc = A.alloc("gb_bc", [128, 2048], F32, ["P7"])
        init_st = A.alloc("init_st", [16, DI], F32, ["P6"])
        ct_t2 = [A.alloc(f"ct_t{i}", [16, 64], BF16, ["P3"]) for i in range(2)]
        bs_t2 = [A.alloc(f"bs_t{i}", [16, 64], BF16, ["P3"]) for i in range(2)]
        bdb2 = [A.alloc(f"bdb{i}", [16, 64], BF16, ["P3"]) for i in range(2)]
        f_t2 = [A.alloc(f"f_t{i}", [16, 64], F32, ["P3"]) for i in range(2)]
        sr_m2 = [A.alloc(f"sr_m{i}", [16, 1], F32, ["P3"]) for i in range(2)]
        sr_p2 = [A.alloc(f"sr_p{i}", [16, 1], F32, ["P3"]) for i in range(2)]
        sr_e2 = [A.alloc(f"sr_e{i}", [16, 1], F32, ["P3"]) for i in range(2)]
        lnv = [A.alloc(f"lnv{i}", [128, 1], F32, ["P0"]) for i in range(6)]
        sc_sb = A.alloc("sc_sb", [128, 8], F32, ["P0", "SC"])

        hi_fm = A.alloc("hi_fm", [128, 8, TOK], BF16, ["P0", "SC", "P3"])
        lo_fm = A.alloc("lo_fm", [128, 8, TOK], BF16, ["P0", "SC"])
        xring = [A.alloc(f"xring{i}", [128, 1024], F32, ["P0"]) for i in range(2)]
        xnf = A.alloc("xnf", [128, 1024], F32, ["P0"])
        hi_t = A.alloc("hi_t", [128, 1024], BF16, ["P0"])
        lo_t = A.alloc("lo_t", [128, 1024], BF16, ["P0"])
        sqdump = A.alloc("sqdump", [128, 1024], F32, ["P0"])

        winr = [A.alloc(f"winr{i}", [128, 8, 128], BF16, ["P3"]) for i in range(4)]
        xinr = [A.alloc(f"xinr{i}", [128, 512], BF16, ["P3"]) for i in range(4)]
        abstr = [A.alloc(f"abstr{i}", [128, TOK], BF16, ["P3"]) for i in range(2)]
        ga_t = A.alloc("ga_t", [128, 512], BF16, ["P3"])
        gb_t = A.alloc("gb_t", [128, 512], BF16, ["P3"])
        a_raw = A.alloc("a_raw", [16, TOK], F32, ["P3"])
        b_raw = A.alloc("b_raw", [16, TOK], F32, ["P3"])
        c_raw = A.alloc("c_raw", [16, TOK], F32, ["P3"])
        se_t = A.alloc("se_t", [16, TOK], F32, ["P3"])
        xtm3 = [A.alloc(f"xtm3_{i}", [128, DI], BF16, ["P3"]) for i in range(2)]

        r_S = A.alloc("r_S", [128, 128, 8], F32, ["P3"])
        r_e = A.alloc("r_e", [128, 128, 8], F32, ["P3"])
        r_oh = A.alloc("r_oh", [128, 16, 128], BF16, ["P3"])
        r_scan = A.alloc("r_scan", [128, 16, 128], F32, ["P3"])
        r_t8 = A.alloc("r_t8", [128, 128, 8], F32, ["P3"])
        r_m = A.alloc("r_m", [128, 128], F32, ["P3"])
        r_ss = A.alloc("r_ss", [128, 128], F32, ["P3"])
        r_g1 = A.alloc("r_g1", [128, 128], F32, ["P3"])
        r_g2 = A.alloc("r_g2", [128, 128], F32, ["P3"])
        r_i1 = A.alloc("r_i1", [128, 128], F32, ["P3"])
        r_i2 = A.alloc("r_i2", [128, 128], F32, ["P3"])
        r_k0 = A.alloc("r_k0", [128, 128], F32, ["P3"])
        r_geJ = A.alloc("r_geJ", [128, 128], F32, ["P3"])
        dbg_rt = A.alloc("dbg_rt", [128, 16, 128], F32, ["P3"]) if DEBUG else None
        r_tot = A.alloc("r_tot", [128, 16], F32, ["P3"])
        r_totb = A.alloc("r_totb", [128, 16], BF16, ["P3"])
        r_base = A.alloc("r_base", [128, 16], F32, ["P3"])
        r_off = A.alloc("r_off", [1, 16], F32, ["P3"])
        r_adj = A.alloc("r_adj", [128, 16], F32, ["P3"])

        wfc1 = A.alloc("wfc1", [128, 8, H], BF16, ["MO"])
        wfc2 = A.alloc("wfc2", [128, 32, D], BF16, ["MO"])
        h_fm = A.alloc("h_fm", [128, 32, MBLK], BF16, ["MO"])
        xfm = [A.alloc(f"xfm{i}", [128, 8, MBLK], BF16, ["MO"]) for i in range(2)]
        y_sb = [A.alloc(f"y_sb{i}", [128, 1024], BF16, ["MO"]) for i in range(2)]

        woutsb = A.alloc("woutsb", [128, 16, D], BF16, ["P6"])
        ssm_tm = A.alloc("ssm_tm", [128, NT, D], BF16, ["P6", "P7"])
        out_nrm = A.alloc("out_nrm", [128, 16, 128], BF16, ["P6"])
        sq_sb = A.alloc("sq_sb", [128, 1024], F32, ["P6"])
        rstd_bc = A.alloc("rstd_bc", [128, 64], F32, ["P6"])
        rms_row = A.alloc("rms_row", [1, 64], F32, ["P6"])
        xtm6 = [A.alloc(f"xtm6_{i}", [128, DI], BF16, ["P6"]) for i in range(2)]
        dbg_so = A.alloc("dbg_so", [128, 1024], BF16, ["P6"]) if DEBUG else None

        moe4 = [A.alloc(f"moe4_{i}", [128, 1024], F32, ["P7"]) for i in range(4)]
        xr4 = [A.alloc(f"xr4_{i}", [128, 1024], F32, ["P7"]) for i in range(4)]
        xnf4 = [A.alloc(f"xnf4_{i}", [128, 1024], F32, ["P7"]) for i in range(4)]
        res4 = [A.alloc(f"res4_{i}", [128, 1024], F32, ["P7"]) for i in range(4)]
        ffn4 = [A.alloc(f"ffn4_{i}", [128, 1024], F32, ["P7"]) for i in range(4)]
        fin4 = [A.alloc(f"fin4_{i}", [128, 1024], F32, ["P7"]) for i in range(4)]
        sqd4 = [A.alloc(f"sqd4_{i}", [128, 1024], F32, ["P7"]) for i in range(4)]
        lnS = [[A.alloc(f"lnS{i}_{j}", [128, 1], F32, ["P7"]) for j in range(6)]
               for i in range(4)]


        top = max(e for (_s, e, _p, _n) in A.items)
        assert top <= SBT, top


        def bd_sl(k):
            return bd_all[(k % 2) * 64:(k % 2) * 64 + 64,
                          (k // 2) * 16:(k // 2) * 16 + 16]

        def m2_sl(k):
            return m2m_all[(k % 2) * 64:(k % 2) * 64 + 64,
                           (k // 2) * 64:(k // 2) * 64 + 64]
        psA = EC(nc.psum_tensor("psA", [128, 512], F32))[:, :]
        psB = EC(nc.psum_tensor("psB", [128, 512], F32))[:, :]
        psY0 = EC(nc.psum_tensor("psY0", [128, 512], F32))[:, :]
        psY1 = EC(nc.psum_tensor("psY1", [128, 512], F32))[:, :]
        psM = EC(nc.psum_tensor("psM", [128, 512], F32))[:, :]
        psO = EC(nc.psum_tensor("psO", [128, 1024], F32))[:, :]
        psT = EC(nc.psum_tensor("psT", [128, 512], F32))[:, :]

        P.op("gpsimd", lambda g: g.iota(icol, pattern=[[0, 1]], base=0,
             channel_multiplier=1, allow_small_or_imprecise_dtypes=True))
        P.op("gpsimd", lambda g: g.iota(iota8, pattern=[[1, 8]], base=0,
             channel_multiplier=0, allow_small_or_imprecise_dtypes=True))
        P.op("gpsimd", lambda g: g.iota(irow128, pattern=[[1, 128]], base=0,
             channel_multiplier=0, allow_small_or_imprecise_dtypes=True))
        P.op("gpsimd", lambda g: g.memset(onesb, 1.0))
        P.op("gpsimd", lambda g: g.memset(onescf, 1.0))
        P.op("gpsimd", lambda g: g.memset(epsln, LN_EPS))
        P.op("gpsimd", lambda g: g.memset(epsrms, RMS_EPS))
        P.op("gpsimd", lambda g: g.memset(zrow.bitcast(BF16), 0.0), inc="gpc")
        P.op("gpsimd", lambda g: g.memset(ones_rf, 1.0))
        P.op("gpsimd", lambda g: g.memset(zeros128, 0.0), inc="gpc")
        P.op("sync", lambda s: s.dma_start(out=constr, in_=consts_in[:, :]), inc="sld", n=16)
        P.op("sync", lambda s: s.dma_start(out=normw, in_=normw_in[:, :]), inc="sld", n=16)
        P.op("sync", lambda s: s.dma_start(out=fc1b_sb, in_=fc1b_in[:, :]), inc="sld", n=16)
        P.op("gpsimd", lambda g: g.dma_start(
            out=wabc_sb, in_=wabc.rearrange("(dc p) w -> p dc w", p=128)), inc="gld", n=16)
        wgf_ld = P.op("gpsimd", lambda g: g.dma_start(
            out=wg_hi, in_=wg_in.rearrange("(dc p) e -> p dc e", p=128)), inc="gld", n=16)
        wg_f = A.alloc("wg_f", [128, 8, 8], F32, ["P0"])
        P.op("sync", lambda s: s.dma_start(
            out=wg_f, in_=wg_in.rearrange("(dc p) e -> p dc e", p=128)), inc="sld", n=16)
        SMALL_LD = P.cnt["sld"]
        P.wait("vector", "gpc")
        P.wait("vector", "sld", SMALL_LD)
        P.wait("vector", "gld")
        P.op("vector", lambda v: v.tensor_scalar(identb, irow128, icol[:, 0:1], None, ALU.is_equal))
        P.op("vector", lambda v: v.tensor_scalar(identf, irow128, icol[:, 0:1], None, ALU.is_equal))
        P.op("vector", lambda v: v.tensor_scalar(tril_b, irow128, icol[:, 0:1], None, ALU.is_ge), inc="dve")
        P.op("vector", lambda v: v.tensor_tensor(wg_lo, wg_f, wg_hi, ALU.subtract), inc="dve")
        WG_DONE = P.cnt["dve"]
        P.wait("tensor", "sld", SMALL_LD)
        P.wait("tensor", "gpc")
        pv = P.op("tensor", lambda t: t.matmul(psT[:, 0:8], ones_rf, constr, start=True, stop=True), inc="pe")
        P.wait("vector", "pe", pv)
        P.op("vector", lambda v: v.tensor_copy(isoddv, psT[:, 0:1]))
        P.op("vector", lambda v: v.tensor_copy(myev, psT[:, 1:2]), inc="dve")
        P.set_gate("psT", "dve", P.cnt["dve"])

        P.wait("scalar", "gpc")
        for zb in range(CAP // 128):
            P.op("scalar", lambda s, zb=zb: s.dma_start(
                out=d_disp[zb * 128:(zb + 1) * 128, :], in_=zrow.bitcast(BF16)),
                inc="dzr", n=16)

        ln_done = {}
        for i in range(NT):
            sl = lxs[i % 2]
            if i >= 2:
                P.wait("sync", "dve", ln_done[i - 2])
            P.op("sync", lambda s, i=i: s.dma_start(
                out=xring[i % 2], in_=x_in[i * 128:(i + 1) * 128, :]), inc=sl, n=16)
            P.wait("scalar", sl)
            P.gate("scalar", "lnv0")
            P.op("scalar", lambda s, b=xring[i % 2]: s.activation(
                sqdump, b, AF.Square, accum_out=lnv[0]), inc="acts")
            ac_sq = P.cnt["acts"]
            P.wait("vector", sl)
            P.op("vector", lambda v, b=xring[i % 2]: v.tensor_reduce(
                lnv[1], b, AX.X, ALU.add), inc="dve")
            P.wait("vector", "dve")
            P.op("vector", lambda v: v.tensor_scalar(lnv[2], lnv[1], 1.0 / D, None, ALU.mult), inc="dve")
            P.wait("vector", "dve")
            P.op("vector", lambda v: v.tensor_tensor(lnv[3], lnv[2], lnv[2], ALU.mult), inc="dve")
            P.wait("vector", "acts", ac_sq)
            P.wait("vector", "dve")
            dv = P.op("vector", lambda v: v.scalar_tensor_tensor(
                lnv[4], lnv[0], 1.0 / D, lnv[3], ALU.mult, ALU.subtract), inc="dve")
            P.set_gate("lnv0", "dve", dv)
            P.wait("scalar", "dve", dv)
            P.op("scalar", lambda s: s.activation(lnv[5], lnv[4], AF.Sqrt, bias=epsln[:, 0:1]), inc="acts")
            P.wait("vector", "acts")
            P.op("vector", lambda v: v.reciprocal(lnv[4], lnv[5]), inc="dve")
            P.wait("vector", "dve")
            P.op("vector", lambda v, b=xring[i % 2]: v.tensor_scalar(
                xnf, b, lnv[2][:, 0:1], lnv[4][:, 0:1], ALU.subtract, ALU.mult), inc="dve")
            P.wait("vector", "dve")
            P.gate("vector", "hilo")
            P.op("vector", lambda v: v.tensor_copy(hi_t, xnf), inc="dve")
            P.wait("vector", "dve")
            dv = P.op("vector", lambda v: v.tensor_tensor(lo_t, xnf, hi_t, ALU.subtract), inc="dve")
            ln_done[i] = dv
            P.wait("sync", "dve", dv)
            P.op("sync", lambda s, i=i: s.dma_start(
                out=d_xnb[i * 128:(i + 1) * 128, :], in_=hi_t), inc="xns", n=16)
            sv = P.op("sync", lambda s, i=i: s.dma_start(
                out=d_xnlo[i * 128:(i + 1) * 128, :], in_=lo_t), inc="xns", n=16)
            P.set_gate("hilo", "xns", sv)

        P.wait("sync", "xns")
        for dc in range(8):
            P.op("sync", lambda s, dc=dc: s.dma_start(
                out=hi_fm[:, dc, :], in_=d_xnb[:, dc * 128:(dc + 1) * 128],
                transpose=True), inc="sld", n=16)
            P.op("sync", lambda s, dc=dc: s.dma_start(
                out=lo_fm[:, dc, :], in_=d_xnlo[:, dc * 128:(dc + 1) * 128],
                transpose=True), inc="sld", n=16)
        FM_LD = P.cnt["sld"]

        P.wait("tensor", "sld", FM_LD)
        P.wait("tensor", "dve", WG_DONE)
        for i in range(NT):
            P.gate("tensor", "psM")
            nmm = 0
            for (af, bf) in ((hi_fm, wg_hi), (hi_fm, wg_lo), (lo_fm, wg_hi)):
                for dc in range(8):
                    nmm += 1
                    pv = P.op("tensor", lambda t, af=af, bf=bf, dc=dc, i=i,
                              st=(nmm == 1), sp=(nmm == 24):
                              t.matmul(psM[:, 0:8], af[:, dc, i * 128:(i + 1) * 128],
                                       bf[:, dc, :], start=st, stop=sp),
                              inc="pe" if nmm == 24 else None)
            P.wait("vector", "pe", pv)
            P.gate("vector", "sc_sb")
            dv = P.op("vector", lambda v: v.tensor_copy(sc_sb, psM[:, 0:8]), inc="dve")
            P.set_gate("psM", "dve", dv)
            P.wait("sync", "dve", dv)
            sv = P.op("sync", lambda s, i=i: s.dma_start(
                out=d_scl[i * 128:(i + 1) * 128, :], in_=sc_sb), inc="scs", n=16)
            P.set_gate("sc_sb", "scs", sv)

        SCORES_PE = P.cnt["pe"]

        P.wait("gpsimd", "xns")
        P.wait("gpsimd", "scs")
        P.op("gpsimd", lambda g: g.collective_compute(
            "AllGather", ALU.bypass, replica_groups=[list(range(8))],
            ins=[d_xnb], outs=[d_xn_all]), inc="cc")
        P.op("gpsimd", lambda g: g.collective_compute(
            "AllGather", ALU.bypass, replica_groups=[list(range(8))],
            ins=[d_scl], outs=[d_sc_all]), inc="cc")
        CC_AG = P.cnt["cc"]
        P.wait("sync", "cc", CC_AG)
        P.op("sync", lambda s: s.dma_start(
            out=r_S.rearrange("p a b -> p (a b)"),
            in_=d_sc_all.rearrange("(p n) e -> p (n e)", p=128)), inc="sld", n=16)
        SC_LD = P.cnt["sld"]
        i8b = iota8.rearrange("p e -> p () e").broadcast_to((128, 128, 8))

        def bc1(t2):
            return t2.rearrange("p a -> p a ()").broadcast_to((128, 128, 8))
        P.wait("vector", "sld", SC_LD)
        P.wait("vector", "pe", SCORES_PE)
        P.op("vector", lambda v: v.tensor_reduce(r_m, r_S, AX.X, ALU.max, negate=True), inc="dve")
        P.wait("vector", "dve")
        P.op("vector", lambda v: v.tensor_tensor(r_e, r_S, bc1(r_m), ALU.add), inc="dve")
        P.wait("scalar", "dve")
        P.op("scalar", lambda s: s.activation(
            r_e.rearrange("p a b -> p (a b)"), r_e.rearrange("p a b -> p (a b)"),
            AF.Exp), inc="acts")
        P.wait("vector", "acts")
        P.op("vector", lambda v: v.tensor_reduce(r_ss, r_e, AX.X, ALU.add), inc="dve")
        P.wait("vector", "dve")
        P.op("vector", lambda v: v.reciprocal(r_m, r_ss), inc="dve")
        P.wait("vector", "dve")
        P.op("vector", lambda v: v.tensor_tensor(r_e, r_e, bc1(r_m), ALU.mult), inc="dve")
        P.wait("vector", "dve")
        P.op("vector", lambda v: v.tensor_reduce(r_g1, r_e, AX.X, ALU.max), inc="dve")
        P.wait("vector", "dve")
        P.op("vector", lambda v: v.tensor_tensor(r_t8, r_e, bc1(r_g1), ALU.is_lt), inc="dve")
        P.wait("vector", "dve")
        P.op("vector", lambda v: v.scalar_tensor_tensor(
            r_t8, r_t8, 100.0, i8b, ALU.mult, ALU.add), inc="dve")
        P.wait("vector", "dve")
        P.op("vector", lambda v: v.tensor_reduce(r_i1, r_t8, AX.X, ALU.min), inc="dve")
        P.wait("vector", "dve")
        oh_r0 = r_oh[:, 0:8, :].rearrange("p c j -> p j c")
        oh_r1 = r_oh[:, 8:16, :].rearrange("p c j -> p j c")
        P.op("vector", lambda v: v.tensor_tensor(oh_r0, i8b, bc1(r_i1), ALU.is_equal), inc="dve")
        P.wait("vector", "dve")
        P.op("vector", lambda v: v.scalar_tensor_tensor(
            r_t8, oh_r0, -2.0, r_e, ALU.mult, ALU.add), inc="dve")
        P.wait("vector", "dve")
        P.op("vector", lambda v: v.tensor_reduce(r_g2, r_t8, AX.X, ALU.max), inc="dve")
        P.wait("vector", "dve")
        P.op("vector", lambda v: v.tensor_tensor(r_t8, r_t8, bc1(r_g2), ALU.is_lt), inc="dve")
        P.wait("vector", "dve")
        P.op("vector", lambda v: v.scalar_tensor_tensor(
            r_t8, r_t8, 100.0, i8b, ALU.mult, ALU.add), inc="dve")
        P.wait("vector", "dve")
        P.op("vector", lambda v: v.tensor_reduce(r_i2, r_t8, AX.X, ALU.min), inc="dve")
        P.wait("vector", "dve")
        P.op("vector", lambda v: v.tensor_tensor(oh_r1, i8b, bc1(r_i2), ALU.is_equal), inc="dve")
        P.wait("vector", "dve")
        P.op("vector", lambda v: v.tensor_tensor(r_ss, r_g1, r_g2, ALU.add), inc="dve")
        P.wait("vector", "dve")
        P.op("vector", lambda v: v.tensor_scalar(r_ss, r_ss, EPS_TOP, None, ALU.max), inc="dve")
        P.wait("vector", "dve")
        P.op("vector", lambda v: v.reciprocal(r_m, r_ss), inc="dve")
        P.wait("vector", "dve")
        P.op("vector", lambda v: v.tensor_tensor(r_g1, r_g1, r_m, ALU.mult))
        P.op("vector", lambda v: v.tensor_tensor(r_g2, r_g2, r_m, ALU.mult), inc="dve")
        if DEBUG and DEBUG_INLINE:
            P.wait("vector", "dve")
            P.op("vector", lambda v: v.tensor_copy(dbg_rt[:, 0, :], r_g1))
            P.op("vector", lambda v: v.tensor_copy(dbg_rt[:, 1, :], r_g2))
            P.op("vector", lambda v: v.tensor_copy(dbg_rt[:, 2, :], r_i1))
            P.op("vector", lambda v: v.tensor_copy(dbg_rt[:, 3, :], r_i2), inc="dve")
        P.wait("vector", "dve")
        for ch in range(16):
            P.op("vector", lambda v, ch=ch: v.tensor_tensor_scan(
                r_scan[:, ch, :], zeros128[:, 0:1].broadcast_to((128, 128)),
                r_oh[:, ch, :], 0.0, ALU.add, ALU.add),
                inc="dve" if ch == 15 else None)
        P.wait("vector", "dve")
        P.op("vector", lambda v: v.tensor_copy(
            r_tot, r_scan[:, :, 127:128].rearrange("p c a -> p (c a)")), inc="dve")
        P.wait("vector", "dve")
        P.op("vector", lambda v: v.tensor_copy(r_totb, r_tot), inc="dve")
        P.wait("tensor", "dve")
        P.gate("tensor", "psT")
        pv = P.op("tensor", lambda t: t.matmul(psT[:, 0:16], tril_b, r_totb,
                                               start=True, stop=True), inc="pe")
        P.wait("vector", "pe", pv)
        P.op("vector", lambda v: v.tensor_tensor(r_base, psT[:, 0:16], r_tot, ALU.subtract), inc="dve")
        pv = P.op("tensor", lambda t: t.matmul(psT[0:16, 16:17], r_totb, onesb,
                                               start=True, stop=True), inc="pe")
        P.wait("vector", "pe", pv)
        P.wait("vector", "dve")
        P.op("vector", lambda v: v.tensor_copy(r_tot[0:16, 0:1], psT[0:16, 16:17]), inc="dve")
        P.wait("tensor", "dve")
        pv = P.op("tensor", lambda t: t.transpose(psT[0:1, 32:48], r_tot[0:16, 0:1],
                                                  identf[0:16, 0:16]), inc="pe")
        P.wait("vector", "pe", pv)
        P.op("vector", lambda v: v.memset(r_off, 0.0), inc="dve")
        P.wait("vector", "dve")
        P.op("vector", lambda v: v.tensor_copy(r_off[:, 8:16], psT[0:1, 32:40]), inc="dve")
        P.wait("tensor", "dve")
        pv = P.op("tensor", lambda t: t.matmul(psT[:, 48:64], ones_rf, r_off,
                                               start=True, stop=True), inc="pe")
        P.wait("vector", "pe", pv)
        P.op("vector", lambda v: v.scalar_tensor_tensor(
            r_adj, r_base, -1.0, psT[:, 48:64], ALU.add, ALU.add), inc="dve")
        P.wait("vector", "dve")
        P.op("vector", lambda v: v.tensor_tensor(
            r_scan, r_scan,
            r_adj.rearrange("p c -> p c ()").broadcast_to((128, 16, 128)),
            ALU.add), inc="dve")
        P.wait("vector", "dve")
        P.op("vector", lambda v: v.tensor_tensor(
            r_t8, r_scan[:, 0:8, :].rearrange("p c j -> p j c"), oh_r0, ALU.mult), inc="dve")
        P.wait("vector", "dve")
        P.op("vector", lambda v: v.tensor_reduce(r_ss, r_t8, AX.X, ALU.add), inc="dve")
        P.wait("vector", "dve")
        P.op("vector", lambda v: v.tensor_scalar(r_k0, r_ss, float(CAP), None, ALU.is_lt), inc="dve")
        if DEBUG and DEBUG_INLINE:
            P.op("vector", lambda v: v.tensor_copy(dbg_rt[:, 4, :], r_ss))
            P.op("vector", lambda v: v.tensor_copy(dbg_rt[:, 5, :], r_k0), inc="dve")
        P.wait("vector", "dve")
        P.op("vector", lambda v: v.tensor_tensor(r_g1, r_g1, r_k0, ALU.mult), inc="dve")
        P.op("vector", lambda v: v.tensor_tensor(
            r_t8, r_scan[:, 8:16, :].rearrange("p c j -> p j c"), oh_r1, ALU.mult), inc="dve")
        P.wait("vector", "dve")
        P.op("vector", lambda v: v.tensor_reduce(r_ss, r_t8, AX.X, ALU.add), inc="dve")
        P.wait("vector", "dve")
        P.op("vector", lambda v: v.tensor_scalar(r_k0, r_ss, float(CAP), None, ALU.is_lt), inc="dve")
        P.wait("vector", "dve")
        P.op("vector", lambda v: v.tensor_tensor(r_g2, r_g2, r_k0, ALU.mult), inc="dve")
        if DEBUG and DEBUG_INLINE:
            P.op("vector", lambda v: v.tensor_copy(dbg_rt[:, 6, :], r_ss))
            P.op("vector", lambda v: v.tensor_copy(dbg_rt[:, 8, 0:16], r_base), inc="dve")
        P.wait("vector", "dve")
        BIGIDX = 1.0e6
        for (rk, ohr, idr, sct) in ((0, oh_r0, r_i1, scidx0), (1, oh_r1, r_i2, scidx1)):
            sl = r_scan[:, 0:8, :] if rk == 0 else r_scan[:, 8:16, :]
            P.op("vector", lambda v, sl=sl, ohr=ohr: v.tensor_tensor(
                r_t8, sl.rearrange("p c j -> p j c"), ohr, ALU.mult), inc="dve")
            P.wait("vector", "dve")
            P.op("vector", lambda v: v.tensor_reduce(r_ss, r_t8, AX.X, ALU.add), inc="dve")
            P.wait("vector", "dve")
            P.op("vector", lambda v: v.tensor_scalar(r_k0, r_ss, float(CAP), None, ALU.is_lt), inc="dve")
            P.wait("vector", "dve")
            P.op("vector", lambda v, idr=idr: v.tensor_scalar(r_m, idr, myev[:, 0:1], None, ALU.is_equal), inc="dve")
            P.wait("vector", "dve")
            P.op("vector", lambda v: v.tensor_tensor(r_k0, r_k0, r_m, ALU.mult), inc="dve")
            P.wait("vector", "dve")
            P.op("vector", lambda v: v.tensor_scalar(r_ss, r_ss, -BIGIDX, None, ALU.add), inc="dve")
            P.wait("vector", "dve")
            P.op("vector", lambda v: v.scalar_tensor_tensor(
                r_ss, r_ss, BIGIDX, r_k0, ALU.bypass, ALU.mult) if False else
                v.tensor_tensor(r_ss, r_ss, r_k0, ALU.mult), inc="dve")
            P.wait("vector", "dve")
            P.op("vector", lambda v: v.tensor_scalar(r_ss, r_ss, BIGIDX, None, ALU.add), inc="dve")
            P.wait("vector", "dve")
            P.wait("tensor", "dve")
            P.gate("tensor", "psT")
            pv = P.op("tensor", lambda t: t.transpose(psT[:, 0:128], r_ss, identf), inc="pe")
            P.wait("vector", "pe", pv)
            dv = P.op("vector", lambda v, sct=sct: v.tensor_copy(sct, psT[:, 0:128]), inc="dve")
            P.set_gate("psT", "dve", dv)
            P.op("vector", lambda v, sl=sl, ohr=ohr: v.tensor_tensor(
                r_t8, sl.rearrange("p c j -> p j c"), ohr, ALU.mult), inc="dve")
            P.wait("vector", "dve")
            P.op("vector", lambda v: v.tensor_reduce(r_m, r_t8, AX.X, ALU.add), inc="dve")
            P.wait("vector", "dve")
            P.op("vector", lambda v: v.tensor_scalar(r_k0, r_m, float(CAP), None, ALU.is_lt), inc="dve")
            P.wait("vector", "dve")
            P.op("vector", lambda v, idr=idr: v.scalar_tensor_tensor(
                r_ss, idr, float(CAP), r_m, ALU.mult, ALU.add), inc="dve")
            P.wait("vector", "dve")
            dv = P.op("vector", lambda v: v.tensor_tensor(r_ss, r_ss, r_k0, ALU.mult), inc="dve")
            if rk == 0:
                P.op("vector", lambda v: v.tensor_scalar(mycol, myev, 16.0, None, ALU.mult), inc="dve")
                P.wait("vector", "dve")
                P.op("vector", lambda v: v.tensor_tensor(mycol, icol, mycol, ALU.subtract), inc="dve")
                P.wait("vector", "dve")
                for i in range(16):
                    P.op("vector", lambda v, i=i: v.tensor_scalar(
                        ohmy[:, i:i + 1], mycol, float(i), None, ALU.is_equal),
                        inc="dve" if i == 15 else None)
            P.wait("tensor", "dve")
            P.gate("tensor", "psT")
            pv = P.op("tensor", lambda t: t.matmul(psT[:, 0:16], r_ss, ohmy,
                                                   start=True, stop=True), inc="pe")
            P.wait("vector", "pe", pv)
            dv = P.op("vector", lambda v, rk=rk: v.tensor_copy(
                cidx_loc[:, rk * 16:(rk + 1) * 16], psT[:, 0:16]), inc="dve")
            gsrc = r_g1 if rk == 0 else r_g2
            pv = P.op("tensor", lambda t, gsrc=gsrc: t.matmul(psT[:, 16:32], gsrc, ohmy,
                                                              start=True, stop=True), inc="pe")
            P.wait("vector", "pe", pv)
            dv = P.op("vector", lambda v, rk=rk: v.tensor_copy(
                g_loc[:, rk * 16:(rk + 1) * 16], psT[:, 16:32]), inc="dve")
            P.set_gate("psT", "dve", dv)
        RT_DONE = P.cnt["dve"]

        XZ_PS = [(psA, psB, "psA", "psB"), (psY0, psY1, "psY0", "psY1")]
        ABC_PS = [(psO[0:80, 0:512], "psO"), (psO[0:80, 512:1024], "psO"),
                  (psT[0:80, :], "psT"), (psM[0:80, :], "psM")]
        for cp in range(16):
            for half in range(2):
                c = cp + 16 * half
                slot = (2 * cp + half) % 4
                P.gate("gpsimd", f"winr{slot}")
                P.op("gpsimd", lambda g, c=c, slot=slot: g.dma_start(
                    out=winr[slot], in_=w_in[:, c * 128:(c + 1) * 128].rearrange(
                        "(dc p) n -> p dc n", p=128)), inc=wis[slot], n=16)
            for tb in range(4):
                pa, pb, nma, nmb = XZ_PS[tb % 2]
                sla, slb = wis[(2 * cp) % 4], wis[(2 * cp + 1) % 4]
                P.wait("tensor", sla)
                P.wait("tensor", slb)
                P.gate("tensor", nma)
                P.gate("tensor", nmb)
                for dc in range(8):
                    P.op("tensor", lambda t, dc=dc, tb=tb, pa=pa, cp=cp:
                         t.matmul(pa, winr[(2 * cp) % 4][:, dc, :],
                                  hi_fm[:, dc, tb * 512:(tb + 1) * 512],
                                  start=(dc == 0), stop=(dc == 7)))
                for dc in range(8):
                    pv = P.op("tensor", lambda t, dc=dc, tb=tb, pb=pb, cp=cp:
                              t.matmul(pb, winr[(2 * cp + 1) % 4][:, dc, :],
                                       hi_fm[:, dc, tb * 512:(tb + 1) * 512],
                                       start=(dc == 0), stop=(dc == 7)),
                              inc="pe" if dc == 7 else None)
                P.set_gate(f"winr{(2 * cp) % 4}", "pe", pv)
                P.set_gate(f"winr{(2 * cp + 1) % 4}", "pe", pv)
                P.wait("scalar", "pe", pv)
                P.gate("scalar", "ga")
                P.op("scalar", lambda s, pa=pa: s.activation(ga_t, pa, AF.Silu), inc="acts")
                av = P.op("scalar", lambda s, pb=pb: s.activation(gb_t, pb, AF.Sigmoid), inc="acts")
                P.set_gate(nma, "acts", av)
                P.set_gate(nmb, "acts", av)
                xslot = (4 * cp + tb) % 4
                P.wait("vector", "acts", av)
                P.gate("vector", f"xinr{xslot}_st")
                dv = P.op("vector", lambda v, xslot=xslot: v.tensor_tensor(
                    xinr[xslot], ga_t, gb_t, ALU.mult), inc="dve")
                P.set_gate("ga", "dve", dv)
                P.wait("sync", "dve", dv)
                sv = P.op("sync", lambda s, cp=cp, tb=tb, xslot=xslot: s.dma_start(
                    out=d_xin[cp * 128:(cp + 1) * 128, tb * 512:(tb + 1) * 512],
                    in_=xinr[xslot]), inc="xis", n=16)
                P.set_gate(f"xinr{xslot}_st", "xis", sv)
        XZ_PE = P.cnt["pe"]
        P.wait("sync", "xis")
        abs_sems = ["sx0", "sx1"]
        for cp in range(16):
            absl = cp % 2
            P.gate("sync", f"abst{absl}")
            P.op("sync", lambda s, cp=cp, absl=absl: s.dma_start(
                out=abstr[absl], in_=d_xin[cp * 128:(cp + 1) * 128, :]),
                inc=["ab0", "ab1"][absl], n=16)
            P.wait("tensor", ["ab0", "ab1"][absl])
            for tb in range(4):
                abc_ps, abc_nm = ABC_PS[tb]
                if cp == 0:
                    P.gate("tensor", abc_nm)
                pv2 = P.op("tensor", lambda t, cp=cp, tb=tb, absl=absl, abc_ps=abc_ps:
                           t.matmul(abc_ps, wabc_sb[:, cp, :],
                                    abstr[absl][:, tb * 512:(tb + 1) * 512],
                                    start=(cp == 0), stop=(cp == 15)), inc="pe")
            P.set_gate(f"abst{absl}", "pe", pv2)
        ABC_PE = P.cnt["pe"]

        P.wait("vector", "pe", ABC_PE)
        for tb in range(4):
            ps = ABC_PS[tb][0]
            for (p0, dst) in ((0, a_raw), (32, b_raw), (64, c_raw)):
                dv = P.op("vector", lambda v, tb=tb, ps=ps, p0=p0, dst=dst:
                          v.tensor_copy(dst[:, tb * 512:(tb + 1) * 512],
                                        ps[p0:p0 + 16, :]), inc="dve")
        ABC_EV = P.cnt["dve"]
        for nm in ("psO", "psT", "psM"):
            P.set_gate(nm, "dve", ABC_EV)

        a_sl = a_raw
        b_sl = b_raw
        c_sl = c_raw
        P.wait("vector", "dve", ABC_EV)
        P.op("vector", lambda v: v.tensor_scalar(se_t, a_sl, -5.0, 0.0, ALU.max, ALU.min), inc="dve")
        P.wait("scalar", "dve")
        P.op("scalar", lambda s: s.activation(se_t, se_t, AF.Exp), inc="acts")
        P.wait("vector", "acts")
        P.op("vector", lambda v: v.tensor_tensor_scan(
            se_t, zeros128[0:16, 0:1].broadcast_to((16, TOK)), se_t, 0.0,
            ALU.add, ALU.add), inc="dve")

        if DEBUG and DEBUG_INLINE:
            P.wait("sync", "dve")
            P.op("sync", lambda s: s.dma_start(out=o_se[:, :], in_=se_t), inc="dbs", n=16)
        for k in range(NCH):
            s0, s1 = k * L, (k + 1) * L
            sec = se_t[:, s0:s1]
            par2 = k % 2
            ct_t, bs_t, bdb, f_t = ct_t2[par2], bs_t2[par2], bdb2[par2], f_t2[par2]
            sr_m, sr_p, sr_e = sr_m2[par2], sr_p2[par2], sr_e2[par2]
            P.wait("vector", "dve")
            P.gate("vector", f"srefs{par2}")
            ap_mid = se_t[:, k * L + 31:k * L + 32]
            P.op("vector", lambda v, ap=ap_mid, sr_m=sr_m: v.tensor_copy(sr_m, ap))
            if k > 0:
                ap_prev = se_t[:, k * L - 1:k * L]
                P.op("vector", lambda v, ap=ap_prev, sr_p=sr_p: v.tensor_copy(sr_p, ap))
            else:
                P.op("vector", lambda v, sr_p=sr_p: v.memset(sr_p, 0.0))
            P.op("vector", lambda v, k=k, sr_e=sr_e: v.tensor_scalar(
                sr_e, se_t[:, (k + 1) * L - 1:(k + 1) * L], -1.0, None, ALU.mult), inc="dve")
            dv_ref = P.cnt["dve"]
            P.wait("scalar", "dve", dv_ref)
            P.gate("scalar", f"f_t{par2}")
            P.op("scalar", lambda s, sec=sec, f_t=f_t, sr_m=sr_m: s.activation(
                f_t, sec, AF.Exp, bias=sr_m[:, 0:1], scale=-1.0), inc="acts")
            P.wait("vector", "acts")
            dv = P.op("vector", lambda v, k=k, ct_t=ct_t, f_t=f_t: v.tensor_tensor(
                ct_t, c_sl[:, k * L:(k + 1) * L], f_t, ALU.mult), inc="dve")
            P.wait("scalar", "dve", dv)
            P.op("scalar", lambda s, sec=sec, f_t=f_t, sr_p=sr_p: s.activation(
                f_t, sec, AF.Exp, bias=sr_p[:, 0:1], scale=-1.0), inc="acts")
            P.wait("vector", "acts")
            P.op("vector", lambda v, k=k, f_t=f_t: v.tensor_tensor(
                ctp_all[:, k * 64:(k + 1) * 64], c_sl[:, k * L:(k + 1) * L], f_t,
                ALU.mult), inc="dve")
            P.op("vector", lambda v, sr_m=sr_m: v.tensor_scalar(sr_m, sr_m, -1.0, None, ALU.mult), inc="dve")
            P.wait("scalar", "dve")
            P.op("scalar", lambda s, sec=sec, f_t=f_t, sr_m=sr_m: s.activation(
                f_t, sec, AF.Exp, bias=sr_m[:, 0:1], scale=1.0), inc="acts")
            P.wait("vector", "acts")
            P.op("vector", lambda v, k=k, bs_t=bs_t, f_t=f_t: v.tensor_tensor(
                bs_t, b_sl[:, k * L:(k + 1) * L], f_t, ALU.mult), inc="dve")
            P.wait("scalar", "dve")
            P.op("scalar", lambda s, sec=sec, f_t=f_t, sr_e=sr_e: s.activation(
                f_t, sec, AF.Exp, bias=sr_e[:, 0:1], scale=1.0), inc="acts")
            P.wait("vector", "acts")
            P.op("vector", lambda v, k=k, f_t=f_t: v.tensor_tensor(
                f_t, b_sl[:, k * L:(k + 1) * L], f_t, ALU.mult), inc="dve")
            P.wait("vector", "dve")
            dv = P.op("vector", lambda v, bdb=bdb, f_t=f_t: v.tensor_copy(bdb, f_t), inc="dve")
            P.set_gate(f"f_t{par2}", "dve", dv)
            P.wait("scalar", "dve", dv)
            P.op("scalar", lambda s, k=k, sr_p=sr_p, sr_e=sr_e: s.activation(
                pmat[:, k:k + 1], sr_p, AF.Exp, bias=sr_e[:, 0:1], scale=1.0), inc="acts")
            av_p = P.cnt["acts"]
            P.set_gate(f"srefs{par2}", "acts", av_p)
            if k == 0:
                P.gate("tensor", "psT")
            P.gate("tensor", "psTbd")
            P.wait("tensor", "dve", dv)
            tp_out = psT[0:64, 96:104].bitcast(BF16)
            pv = P.op("tensor", lambda t, tp_out=tp_out, bdb=bdb: t.transpose(
                tp_out, bdb, identb[0:16, 0:16]), inc="pe")
            P.wait("vector", "pe", pv)
            dvb = P.op("vector", lambda v, k=k, tp_out=tp_out: v.tensor_copy(
                bd_sl(k), tp_out), inc="dve")
            P.set_gate("psTbd", "dve", dvb)
            P.gate("tensor", "psTm2")
            P.wait("tensor", "dve")
            pv = P.op("tensor", lambda t, bs_t=bs_t, ct_t=ct_t: t.matmul(
                psT[0:64, 0:64], bs_t, ct_t, start=True, stop=True), inc="pe")
            P.wait("vector", "pe", pv)
            dvm = P.op("vector", lambda v, k=k: v.tensor_tensor(
                m2_sl(k), psT[0:64, 0:64],
                tril_b[0:64, 0:64], ALU.mult), inc="dve")
            P.set_gate("psTm2", "dve", dvm)
        P.set_gate("psT", "dve", P.cnt["dve"])

        P.wait("vector", "acts")
        P.op("vector", lambda v: v.memset(stateA, 0.0), inc="dve")
        st_cur, st_nxt = stateA, stateB
        P.wait("sync", "xis")
        rec_dv = {}
        for k in range(NCH):
            xsl = (k // 2) % 2
            pb = (k % 2) * 64
            if k % 2 == 0:
                P.gate("sync", f"xtm3_{xsl}")
                P.op("sync", lambda s, k=k, xsl=xsl: s.dma_start(
                    out=xtm3[xsl], in_=d_xin[:, k * L:(k + 2) * L], transpose=True),
                    inc=xts[xsl], n=16)
            P.wait("tensor", xts[xsl])
            P.wait("tensor", "dve")
            for dq in range(4):
                pd, pdn = ((psY0, "psY0"), (psY1, "psY1"))[dq % 2]
                P.gate("tensor", pdn)
                pv = P.op("tensor", lambda t, k=k, dq=dq, xsl=xsl, pd=pd, pb=pb: t.matmul(
                    pd[0:16, :], bd_sl(k),
                    xtm3[xsl][pb:pb + 64, dq * 512:(dq + 1) * 512], start=True, stop=True), inc="pe")
                if dq == 3 and k % 2 == 1:
                    P.set_gate(f"xtm3_{xsl}", "pe", pv)
                P.wait("vector", "pe", pv)
                dv = P.op("vector", lambda v, k=k, dq=dq, pd=pd, sc=st_cur, sn=st_nxt:
                          v.scalar_tensor_tensor(
                              sn[:, dq * 512:(dq + 1) * 512],
                              sc[:, dq * 512:(dq + 1) * 512],
                              pmat[:, k:k + 1], pd[0:16, :], ALU.mult, ALU.add), inc="dve")
                P.set_gate(pdn, "dve", dv)
            st_cur, st_nxt = st_nxt, st_cur
        P.wait("sync", "dve")
        P3_PE = P.cnt["pe"]
        P3_DVE = P.cnt["dve"]
        P3_ACT = P.cnt["acts"]
        sv = P.op("sync", lambda s, fs=st_cur: s.dma_start(out=d_state[:, :], in_=fs), inc="sts", n=16)



        if DEBUG and DEBUG_INLINE:
            P.wait("sync", "dve", RT_DONE)
            P.op("sync", lambda s: s.dma_start(out=o_rt[:, :], in_=dbg_rt.rearrange("p a b -> p (a b)")), inc="dbs", n=16)

        P.wait("gpsimd", "dve", RT_DONE)
        P.wait("gpsimd", "cc", CC_AG)
        P.wait("gpsimd", "dzr")
        REGH = {}

        def _mkreg(g):
            r = g.alloc_register()
            g.reg_mov(r, CAP - 1)
            REGH["b"] = r
        P.op("gpsimd", _mkreg)
        for gtb in range(32):
            ssl = gtb % 2
            sxs = ["sx0", "sx1"][ssl]
            P.gate("gpsimd", f"sxn{ssl}")
            P.op("gpsimd", lambda g, gtb=gtb, ssl=ssl: g.dma_start(
                out=sxn[ssl], in_=d_xn_all[gtb * 512:(gtb + 1) * 512, :].rearrange(
                    "(t p) d -> p t d", p=128)), inc=sxs, n=16)
            P.wait("gpsimd", sxs)
            for t4 in range(4):
                gt = gtb * 4 + t4
                for rk, sct in ((0, scidx0), (1, scidx1)):
                    pv = P.op("gpsimd", lambda g, gt=gt, t4=t4, ssl=ssl, sct=sct: g.indirect_dma_start(
                        out=d_disp, out_offset=bass.IndirectOffsetOnAxis(ap=sct[:, gt:gt + 1], axis=0),
                        in_=sxn[ssl][:, t4, :], in_offset=None,
                        bounds_check=REGH["b"], oob_is_err=False), inc="dsp", n=16)
            P.set_gate(f"sxn{ssl}", "dsp", pv)
        DISP_DONE = P.cnt["dsp"]
        P.wait("gpsimd", "sts")
        P.op("gpsimd", lambda g: g.collective_compute(
            "AllGather", ALU.bypass,
            replica_groups=[[0, 1], [2, 3], [4, 5], [6, 7]],
            ins=[d_state], outs=[d_state_ag]), inc="cc")
        CC_ST = P.cnt["cc"]
        P.wait("gpsimd", "pe", P3_PE)
        P.wait("gpsimd", "dve", P3_DVE)
        P.wait("gpsimd", "acts", P3_ACT)
        P.wait("gpsimd", "xt0")
        P.wait("gpsimd", "xt1")
        P.wait("gpsimd", "xis")
        for ch in range(8):
            P.op("gpsimd", lambda g, ch=ch: g.dma_start(
                out=wfc1[:, :, ch * 512:(ch + 1) * 512],
                in_=fc1t_in[:, ch * 512:(ch + 1) * 512].rearrange(
                    "(dc p) n -> p dc n", p=128)), inc="wf1", n=16)
        WFC1_LD = P.cnt["wf1"]
        P.wait("gpsimd", "dve", RT_DONE)
        for ch in range(8):
            P.op("gpsimd", lambda g, ch=ch: g.dma_start(
                out=wfc2[:, ch * 4:(ch + 1) * 4, :],
                in_=fc2_in[ch * 512:(ch + 1) * 512, :].rearrange(
                    "(hc p) d -> p hc d", p=128)), inc="wf2", n=16)
        WFC2_LD = P.cnt["wf2"]

        P.wait("sync", "dve", P3_DVE)
        P.wait("sync", "sts")
        P.wait("sync", "xis")
        sv = P.op("sync", lambda s: s.dma_start(out=fc2b_row, in_=fc2b_in[:, :]), inc="fcb", n=16)
        P.wait("tensor", "fcb")
        P.gate("tensor", "psO")
        P.op("tensor", lambda t: t.matmul(psO[:, 0:512], ones_rf, fc2b_row[:, 0:512], start=True, stop=True))
        pv = P.op("tensor", lambda t: t.matmul(psO[:, 512:1024], ones_rf, fc2b_row[:, 512:1024], start=True, stop=True), inc="pe")
        P.wait("vector", "pe", pv)
        P.wait("vector", "sts")
        P.wait("vector", "xis")
        P.wait("vector", "xt0")
        P.wait("vector", "xt1")
        dv = P.op("vector", lambda v: v.tensor_copy(fc2b_bc, psO[:, :]), inc="dve")
        P.set_gate("psO", "dve", dv)
        FC2B_BC = dv

        P.wait("tensor", "wf1", WFC1_LD)
        for blk in range(NBLK):
            xsl = blk % 2
            xfs = ["xf0", "xf1"][xsl]
            P.gate("sync", f"xfm{xsl}")
            if blk == 0:
                P.wait("sync", "dsp", DISP_DONE)
                P.wait("sync", "pe", P3_PE)
                P.wait("sync", "dve", P3_DVE)
                P.wait("sync", "acts", P3_ACT)
                P.wait("sync", "xt0")
                P.wait("sync", "xt1")
                P.wait("scalar", "dve", P3_DVE)
                P.wait("scalar", "pe", P3_PE)
                P.wait("scalar", "sts")
                P.wait("scalar", "xis")
            for dc in range(8):
                P.op("sync", lambda s, blk=blk, dc=dc, xsl=xsl: s.dma_start(
                    out=xfm[xsl][:, dc, :],
                    in_=d_disp[blk * MBLK:(blk + 1) * MBLK, dc * 128:(dc + 1) * 128],
                    transpose=True), inc=xfs, n=16)
            P.wait("tensor", xfs)
            pe_last = None
            for ht in range(32):
                pp, ppn = ((psA, "psA"), (psB, "psB"))[ht % 2]
                P.gate("tensor", ppn)
                if ht == 0:
                    P.gate("tensor", "h_fm")
                for dc in range(8):
                    pv = P.op("tensor", lambda t, ht=ht, dc=dc, pp=pp, xsl=xsl: t.matmul(
                        pp[:, 0:MBLK], wfc1[:, dc, ht * 128:(ht + 1) * 128],
                        xfm[xsl][:, dc, :], start=(dc == 0), stop=(dc == 7)),
                        inc="pe" if dc == 7 else None)
                pe_last = pv
                P.wait("scalar", "pe", pv)
                av = P.op("scalar", lambda s, ht=ht, pp=pp: s.activation(
                    h_fm[:, ht, :], pp[:, 0:MBLK], AF.Silu,
                    bias=fc1b_sb[:, ht:ht + 1]), inc="acts")
                P.set_gate(ppn, "acts", av)
            P.set_gate(f"xfm{xsl}", "pe", pe_last)
            P.wait("tensor", "acts")
            if blk == 0:
                P.wait("tensor", "wf2", WFC2_LD)
            for st2 in range(2):
                tt = blk * 2 + st2
                zsl = st2 % 2
                for half in range(2):
                    pq, pqn = ((psY0, "psY0"), (psY1, "psY1"))[half]
                    P.gate("tensor", pqn)
                    for hc in range(32):
                        pv = P.op("tensor", lambda t, st2=st2, half=half, hc=hc, pq=pq:
                                  t.matmul(pq, h_fm[:, hc, st2 * 128:(st2 + 1) * 128],
                                           wfc2[:, hc, half * 512:(half + 1) * 512],
                                           start=(hc == 0), stop=(hc == 31)),
                                  inc="pe" if hc == 31 else None)
                    P.wait("vector", "pe", pv)
                    if blk == 0 and st2 == 0 and half == 0:
                        P.wait("vector", "dve", FC2B_BC)
                    P.gate("vector", f"z_sb{zsl}")
                    dv = P.op("vector", lambda v, half=half, zsl=zsl, pq=pq: v.tensor_tensor(
                        y_sb[zsl][:, half * 512:(half + 1) * 512], pq,
                        fc2b_bc[:, half * 512:(half + 1) * 512], ALU.add), inc="dve")
                    P.set_gate(pqn, "dve", dv)
                P.wait("sync", "dve", dv)
                sv = P.op("sync", lambda s, tt=tt, zsl=zsl: s.dma_start(
                    out=d_y[tt * 128:(tt + 1) * 128, :], in_=y_sb[zsl]), inc="yst", n=16)
                P.set_gate(f"z_sb{zsl}", "yst", sv)
            P.set_gate("h_fm", "pe", P.cnt["pe"])
        Y_ST = P.cnt["yst"]
        MO_PE = P.cnt["pe"]

        P.wait("gpsimd", "yst")
        P.op("gpsimd", lambda g: g.collective_compute(
            "AllGather", ALU.bypass, replica_groups=[list(range(8))],
            ins=[d_y], outs=[d_y_all]), inc="cc")
        CC_Y = P.cnt["cc"]

        P.wait("gpsimd", "pe", MO_PE)
        P.wait("gpsimd", "yst")
        P.wait("scalar", "yst")
        for ch in range(4):
            P.op("gpsimd", lambda g, ch=ch: g.dma_start(
                out=woutsb[:, ch * 4:(ch + 1) * 4, :],
                in_=w_out[ch * 512:(ch + 1) * 512, :].rearrange(
                    "(dc p) d -> p dc d", p=128)), inc="wo", n=16)
        WOUT_LD = P.cnt["wo"]
        P.wait("sync", "cc", CC_ST)
        P.wait("sync", "yst")
        iv = P.op("sync", lambda s: s.dma_start(out=init_st, in_=d_state_ag[0:NS, :]), inc="ini", n=16)
        P.wait("vector", "ini")
        P.wait("vector", "yst")
        P.wait("vector", "pe", MO_PE)
        P.op("vector", lambda v: v.tensor_scalar(
            init_st, init_st, isoddv[0:16, 0:1], None, ALU.mult), inc="dve")
        P.wait("vector", "dve")
        P.op("vector", lambda v: v.tensor_copy(st6A, init_st), inc="dve")
        st_cur, st_nxt = st6A, st6B
        P.wait("tensor", "wo", WOUT_LD)
        rec_last = P.cnt["dve"]
        ev_gate = {0: None, 1: None}
        for k in range(NCH):
            xsl = (k // 2) % 2
            pb = (k % 2) * 64
            par = (k % 2) if P6PP else 0

            def po(j, k=k):
                if (k % 2 == 0) or not P6PP:
                    return psO[:, j * 64:(j + 1) * 64]
                return (psA if j < 8 else psB)[:, (j % 8) * 64:(j % 8 + 1) * 64]
            if k % 2 == 0:
                P.gate("sync", f"xtm6_{xsl}")
                P.op("sync", lambda s, k=k, xsl=xsl: s.dma_start(
                    out=xtm6[xsl], in_=d_xin[:, k * L:(k + 2) * L], transpose=True),
                    inc=xts[2 + xsl], n=16)
            P.wait("tensor", xts[2 + xsl])
            P.wait("tensor", "dve", rec_last)
            if par == 0:
                P.gate("tensor", "psO")
            else:
                P.gate("tensor", "psA")
                P.gate("tensor", "psB")
            for j in range(16):
                P.op("tensor", lambda t, k=k, j=j, xsl=xsl, pb=pb, po=po,
                     st=(j in (0, 8)): t.matmul(
                    po(j), xtm6[xsl][pb:pb + 64, j * 128:(j + 1) * 128],
                    m2_sl(k), start=st, stop=False))
            for j in range(16):
                pv = P.op("tensor", lambda t, k=k, j=j, sc=st_cur, po=po,
                          sp=(j in (7, 15)): t.matmul(
                    po(j), sc[:, j * 128:(j + 1) * 128],
                    ctp_all[:, k * 64:(k + 1) * 64], start=False, stop=sp),
                    inc="pe" if j == 15 else None)
            out_pe = pv
            for dq in range(4):
                pd, pdn = ((psY0, "psY0"), (psY1, "psY1"))[dq % 2]
                P.gate("tensor", pdn)
                pv = P.op("tensor", lambda t, k=k, dq=dq, xsl=xsl, pd=pd, pb=pb: t.matmul(
                    pd[0:16, :], bd_sl(k),
                    xtm6[xsl][pb:pb + 64, dq * 512:(dq + 1) * 512], start=True, stop=True), inc="pe")
                if dq == 3 and k % 2 == 1:
                    P.set_gate(f"xtm6_{xsl}", "pe", pv)
                P.wait("vector", "pe", pv)
                dv = P.op("vector", lambda v, k=k, dq=dq, pd=pd, sc=st_cur, sn=st_nxt:
                          v.scalar_tensor_tensor(
                              sn[:, dq * 512:(dq + 1) * 512],
                              sc[:, dq * 512:(dq + 1) * 512],
                              pmat[:, k:k + 1], pd[0:16, :], ALU.mult, ALU.add), inc="dve")
                P.set_gate(pdn, "dve", dv)
            rec_last = dv
            st_cur, st_nxt = st_nxt, st_cur
            P.wait("scalar", "pe", out_pe)
            P.gate("scalar", "sq_sb")
            for j in range(16):
                av = P.op("scalar", lambda s, j=j, k=k, po=po: s.activation(
                    sq_sb[:, j * 64:(j + 1) * 64], po(j),
                    AF.Square), inc="acts" if j == 15 else None)
            P.wait("tensor", "acts", av)
            P.gate("tensor", "psM")
            for j in range(16):
                pv = P.op("tensor", lambda t, j=j: t.matmul(
                    psM[0:1, 0:64], onescf,
                    sq_sb[:, j * 64:(j + 1) * 64], start=(j == 0), stop=(j == 15)),
                    inc="pe" if j == 15 else None)
            P.set_gate("sq_sb", "pe", pv)
            P.wait("scalar", "pe", pv)
            P.gate("scalar", "rms_row")
            P.op("scalar", lambda s: s.activation(
                rms_row, psM[0:1, 0:64], AF.Ln, scale=1.0 / DI, bias=epsrms[0:1, 0:1]), inc="acts")
            P.set_gate("psM", "acts", P.cnt["acts"])
            P.wait("scalar", "acts")
            P.op("scalar", lambda s: s.activation(
                rms_row, rms_row, AF.Exp, scale=-0.5), inc="acts")
            P.wait("tensor", "acts")
            P.gate("tensor", "psT")
            pv = P.op("tensor", lambda t: t.matmul(
                psT[:, 128:192], ones_rf, rms_row, start=True, stop=True), inc="pe")
            P.set_gate("rms_row", "pe", pv)
            P.wait("vector", "pe", pv)
            P.gate("vector", "rstd_bc")
            P.op("vector", lambda v: v.tensor_copy(rstd_bc, psT[:, 128:192]), inc="dve")
            P.set_gate("psT", "dve", P.cnt["dve"])
            P.wait("vector", "dve")
            P.gate("vector", "out_nrm")
            colk = (k % 2) * 64
            for j in range(16):
                dv = P.op("vector", lambda v, j=j, colk=colk, k=k, po=po: v.scalar_tensor_tensor(
                    out_nrm[:, j, colk:colk + 64], po(j),
                    normw[:, j:j + 1], rstd_bc, ALU.mult, ALU.mult),
                    inc="dve" if j == 15 else None)
            P.set_gate("psO" if par == 0 else "psA", "dve", dv)
            P.set_gate("rstd_bc", "dve", dv)
            if k % 2 == 1:
                P.wait("tensor", "dve", dv)
                for half in range(2):
                    pq, pqn = ((psY0, "psY0"), (psY1, "psY1"))[half]
                    P.gate("tensor", pqn)
                    for dc in range(16):
                        pv = P.op("tensor", lambda t, dc=dc, half=half, pq=pq: t.matmul(
                            pq, out_nrm[:, dc, :],
                            woutsb[:, dc, half * 512:(half + 1) * 512],
                            start=(dc == 0), stop=(dc == 15)),
                            inc="pe" if dc == 15 else None)
                    P.wait("vector", "pe", pv)
                    dv2 = P.op("vector", lambda v, k=k, half=half, pq=pq: v.tensor_copy(
                        ssm_tm[:, k // 2, half * 512:(half + 1) * 512], pq), inc="dve")
                    P.set_gate(pqn, "dve", dv2)
                P.set_gate("out_nrm", "pe", pv)
        SSM_DV = P.cnt["dve"]
        P6_PE = P.cnt["pe"]

        P.wait("sync", "cc", CC_Y)
        P.wait("sync", "pe", P6_PE)
        P.wait("scalar", "pe", P6_PE)
        P.wait("gpsimd", "cc", CC_Y)
        P.wait("gpsimd", "pe", P6_PE)
        if DEBUG and DEBUG_INLINE:
            P.op("sync", lambda s: s.dma_start(out=o_m2[:, :], in_=m2m_all), inc="dbs", n=16)
            P.op("sync", lambda s: s.dma_start(out=o_ctp[:, :], in_=ctp_all), inc="dbs", n=16)
            P.op("sync", lambda s: s.dma_start(out=o_bd[:, :], in_=bd_all), inc="dbs", n=16)
            P.op("sync", lambda s: s.dma_start(out=o_pm[:, :], in_=pmat), inc="dbs", n=16)
        sv = P.op("sync", lambda s: s.dma_start(out=gam_row, in_=gamma_in[:, :]), inc="gbl", n=16)
        sv = P.op("sync", lambda s: s.dma_start(out=bet_row, in_=beta_in[:, :]), inc="gbl", n=16)
        P.wait("tensor", "gbl")
        P.wait("tensor", "pe", P6_PE)
        for (row, dst) in ((gam_row, gb_bc[:, 0:1024]), (bet_row, gb_bc[:, 1024:2048])):
            P.gate("tensor", "psO")
            P.op("tensor", lambda t, row=row: t.matmul(psO[:, 0:512], ones_rf, row[:, 0:512], start=True, stop=True))
            pv = P.op("tensor", lambda t, row=row: t.matmul(psO[:, 512:1024], ones_rf, row[:, 512:1024], start=True, stop=True), inc="pe")
            P.wait("vector", "pe", pv)
            dv = P.op("vector", lambda v, dst=dst: v.tensor_copy(dst, psO[:, :]), inc="dve")
            P.set_gate("psO", "dve", dv)
        GB_BC = P.cnt["dve"]
        cgs = ["cg0", "cg1", "cg2", "cg3", "cg4", "cg5", "cg6", "cg7"]
        G = 4
        last_out = {}

        def stepA(i, bx):
            q = i % G
            P.wait("scalar", lxs4[q])
            P.wait("scalar", "acts")
            P.op("scalar", lambda s, bx=bx, q=q: s.activation(
                sqd4[q], bx, AF.Square, accum_out=lnS[q][0]), inc="acts")
            P.wait("vector", lxs4[q])
            P.op("vector", lambda v, bx=bx, q=q: v.tensor_reduce(
                lnS[q][1], bx, AX.X, ALU.add), inc="dve")

        def stepB(i):
            q = i % G
            P.wait("vector", "dve")
            P.op("vector", lambda v, q=q: v.tensor_scalar(
                lnS[q][2], lnS[q][1], 1.0 / D, None, ALU.mult), inc="dve")
            P.wait("vector", "dve")
            P.op("vector", lambda v, q=q: v.tensor_tensor(
                lnS[q][3], lnS[q][2], lnS[q][2], ALU.mult), inc="dve")
            P.wait("vector", "acts")
            P.wait("vector", "dve")
            P.op("vector", lambda v, q=q: v.scalar_tensor_tensor(
                lnS[q][4], lnS[q][0], 1.0 / D, lnS[q][3], ALU.mult, ALU.subtract), inc="dve")

        def stepC(i):
            q = i % G
            P.wait("scalar", "dve")
            P.op("scalar", lambda s, q=q: s.activation(
                lnS[q][5], lnS[q][4], AF.Sqrt, bias=epsln[:, 0:1]), inc="acts")

        def stepD(i, bx):
            q = i % G
            P.wait("vector", "acts")
            P.op("vector", lambda v, q=q: v.reciprocal(lnS[q][4], lnS[q][5]), inc="dve")
            P.wait("vector", "dve")
            P.op("vector", lambda v, bx=bx, q=q: v.tensor_scalar(
                xnf4[q], bx, lnS[q][2][:, 0:1], lnS[q][4][:, 0:1],
                ALU.subtract, ALU.mult), inc="dve")

        def stepE(i):
            q = i % G
            s0, s1 = (2 * i) % 8, (2 * i + 1) % 8
            P.wait("vector", cgs[s0])
            P.wait("vector", cgs[s1])
            P.wait("vector", "dve")
            P.op("vector", lambda v, i=i, s0=s0, q=q: v.tensor_scalar(
                moe4[q], ybuf[s0], g_loc[:, i:i + 1], None, ALU.mult), inc="dve")
            P.wait("vector", "dve")
            dv = P.op("vector", lambda v, i=i, s1=s1, q=q: v.scalar_tensor_tensor(
                moe4[q], ybuf[s1], g_loc[:, 16 + i:17 + i], moe4[q], ALU.mult, ALU.add), inc="dve")
            P.set_gate(f"ybuf{s0}", "dve", dv)
            P.set_gate(f"ybuf{s1}", "dve", dv)
            P.wait("vector", "dve")
            P.op("vector", lambda v, q=q: v.tensor_tensor(
                res4[q], xnf4[q], moe4[q], ALU.add), inc="dve")

        def stepF(i):
            q = i % G
            P.wait("scalar", "dve")
            P.wait("scalar", "acts")
            P.op("scalar", lambda s, q=q: s.activation(
                sqd4[q], res4[q], AF.Square, accum_out=lnS[q][0]), inc="acts")
            P.wait("vector", "dve")
            P.op("vector", lambda v, q=q: v.tensor_reduce(
                lnS[q][1], res4[q], AX.X, ALU.add), inc="dve")

        def stepG(i):
            q = i % G
            P.wait("vector", "dve")
            P.op("vector", lambda v, q=q: v.tensor_scalar(
                lnS[q][2], lnS[q][1], 1.0 / D, None, ALU.mult), inc="dve")
            P.wait("vector", "dve")
            P.op("vector", lambda v, q=q: v.tensor_tensor(
                lnS[q][3], lnS[q][2], lnS[q][2], ALU.mult), inc="dve")
            P.wait("vector", "acts")
            P.wait("vector", "dve")
            P.op("vector", lambda v, q=q: v.scalar_tensor_tensor(
                lnS[q][4], lnS[q][0], 1.0 / D, lnS[q][3], ALU.mult, ALU.subtract), inc="dve")

        def stepH(i):
            q = i % G
            P.wait("scalar", "dve")
            P.op("scalar", lambda s, q=q: s.activation(
                lnS[q][5], lnS[q][4], AF.Sqrt, bias=epsln[:, 0:1]), inc="acts")

        def stepI(i, bx):
            q = i % G
            P.wait("vector", "acts")
            P.op("vector", lambda v, q=q: v.reciprocal(lnS[q][4], lnS[q][5]), inc="dve")
            P.wait("vector", "dve")
            P.op("vector", lambda v, q=q: v.tensor_scalar(
                ffn4[q], res4[q], lnS[q][2][:, 0:1], lnS[q][4][:, 0:1],
                ALU.subtract, ALU.mult), inc="dve")
            P.wait("vector", "dve")
            P.op("vector", lambda v, q=q: v.tensor_tensor(
                ffn4[q], ffn4[q], gb_bc[:, 0:1024], ALU.mult), inc="dve")
            P.wait("vector", "dve")
            P.op("vector", lambda v, q=q: v.tensor_tensor(
                ffn4[q], ffn4[q], gb_bc[:, 1024:2048], ALU.add), inc="dve")
            P.wait("vector", "dve")
            P.wait("vector", "outs")
            P.op("vector", lambda v, i=i, bx=bx, q=q: v.tensor_tensor(
                fin4[q], bx, ssm_tm[:, i, :], ALU.add), inc="dve")
            P.wait("vector", "dve")
            dv = P.op("vector", lambda v, q=q: v.tensor_tensor(
                fin4[q], fin4[q], ffn4[q], ALU.add), inc="dve")
            return dv

        for g in range(NT // G):
            tiles = list(range(g * G, (g + 1) * G))
            for i in tiles:
                q = i % G
                if g >= 1:
                    P.wait("sync", "dve", last_out[i - G])
                P.op("sync", lambda s, i=i, q=q: s.dma_start(
                    out=xr4[q], in_=x_in[i * 128:(i + 1) * 128, :]), inc=lxs4[q], n=16)
                s0, s1 = (2 * i) % 8, (2 * i + 1) % 8
                P.gate("gpsimd", f"ybuf{s0}")
                P.op("gpsimd", lambda g_, i=i, s0=s0: g_.indirect_dma_start(
                    out=ybuf[s0][:, :], out_offset=None, in_=d_y_all,
                    in_offset=bass.IndirectOffsetOnAxis(ap=cidx_loc[:, i:i + 1], axis=0)),
                    inc=cgs[s0], n=16)
                P.gate("gpsimd", f"ybuf{s1}")
                P.op("gpsimd", lambda g_, i=i, s1=s1: g_.indirect_dma_start(
                    out=ybuf[s1][:, :], out_offset=None, in_=d_y_all,
                    in_offset=bass.IndirectOffsetOnAxis(ap=cidx_loc[:, 16 + i:17 + i], axis=0)),
                    inc=cgs[s1], n=16)
            for i in tiles:
                stepA(i, xr4[i % G])
            for i in tiles:
                stepB(i)
            for i in tiles:
                stepC(i)
            for i in tiles:
                stepD(i, xr4[i % G])
            for i in tiles:
                stepE(i)
            for i in tiles:
                stepF(i)
            for i in tiles:
                stepG(i)
            for i in tiles:
                stepH(i)
            for i in tiles:
                dv = stepI(i, xr4[i % G])
                last_out[i] = dv
                P.wait("sync", "dve", dv)
                P.op("sync", lambda s, i=i, q=i % G: s.dma_start(
                    out=o_out[i * 128:(i + 1) * 128, :], in_=fin4[q]), inc="outs", n=16)
        P.wait("sync", "outs")
        P.wait("gpsimd", "outs")

        for name in P.cnt:
            P.sems[name] = EC(nc.semaphore(name))
        block = EC(nc.Block())

        @block.sync
        def _(e):
            for f in P.q["sync"]:
                f(e)

        @block.gpsimd
        def _(e):
            for f in P.q["gpsimd"]:
                f(e)

        @block.vector
        def _(e):
            for f in P.q["vector"]:
                f(e)

        @block.scalar
        def _(e):
            for f in P.q["scalar"]:
                f(e)

        @block.tensor
        def _(e):
            for f in P.q["tensor"]:
                f(e)

    return nc


_NC_CACHE = None
_LAST_RES = None
_LAST_PLAN = None


def kernel(L0_attn_x, L0_attn_W_in, L0_attn_W_A, L0_attn_W_B, L0_attn_W_C,
           L0_attn_W_out, L0_attn_norm_w, L0_ffn_wg, L0_ffn_fc1_w, L0_ffn_fc1_b,
           L0_ffn_fc2_w, L0_ffn_fc2_b, L0_ffn_ln_gamma, L0_ffn_ln_beta):
    global _NC_CACHE
    x = np.ascontiguousarray(np.asarray(L0_attn_x, dtype=np.float32).reshape(NTOK, D))
    w_in_v = np.ascontiguousarray(np.asarray(L0_attn_W_in, dtype=np.float32))
    wa = np.asarray(L0_attn_W_A, dtype=np.float32)
    wb = np.asarray(L0_attn_W_B, dtype=np.float32)
    wc = np.asarray(L0_attn_W_C, dtype=np.float32)
    wabc_v = np.zeros((DI, 80), np.float32)
    wabc_v[:, 0:16] = wa
    wabc_v[:, 32:48] = wb
    wabc_v[:, 64:80] = wc
    w_out_v = np.ascontiguousarray(np.asarray(L0_attn_W_out, dtype=np.float32))
    norm_w_v = np.ascontiguousarray(np.asarray(L0_attn_norm_w, dtype=np.float32))
    wg_v = np.ascontiguousarray(np.asarray(L0_ffn_wg, dtype=np.float32))
    fc1 = np.asarray(L0_ffn_fc1_w, dtype=np.float32)
    fc1t = np.ascontiguousarray(fc1.transpose(0, 2, 1))
    fc1b = np.ascontiguousarray(np.asarray(L0_ffn_fc1_b, dtype=np.float32))
    fc2 = np.ascontiguousarray(np.asarray(L0_ffn_fc2_w, dtype=np.float32))
    fc2b = np.ascontiguousarray(np.asarray(L0_ffn_fc2_b, dtype=np.float32))
    gam = np.ascontiguousarray(np.asarray(L0_ffn_ln_gamma, dtype=np.float32).reshape(1, D))
    bet = np.ascontiguousarray(np.asarray(L0_ffn_ln_beta, dtype=np.float32).reshape(1, D))

    if _NC_CACHE is None:
        _NC_CACHE = build()
    nc = _NC_CACHE

    in_maps = []
    for c in range(8):
        consts = np.zeros((1, 8), np.float32)
        consts[0, 0] = float(c % 2)
        consts[0, 1] = float(c)
        in_maps.append({
            "x": np.ascontiguousarray(x[c * TOK:(c + 1) * TOK]),
            "w_in": w_in_v,
            "wabc": wabc_v,
            "w_out": w_out_v,
            "wg": wg_v,
            "norm_w": np.ascontiguousarray(norm_w_v.reshape(16, 128).T),
            "fc1t": np.ascontiguousarray(fc1t[c]),
            "fc1b": np.ascontiguousarray(fc1b[c].reshape(32, 128).T),
            "fc2": np.ascontiguousarray(fc2[c]),
            "fc2b": np.ascontiguousarray(fc2b[c].reshape(1, D)),
            "gamma": gam,
            "beta": bet,
            "consts": consts,
        })
    res = run_bass_kernel_spmd(nc, in_maps, core_ids=list(range(8)))
    global _LAST_RES
    _LAST_RES = res
    out = np.concatenate([np.asarray(r["out"]) for r in res.results], axis=0)
    return out.reshape(B, T, D)
